# version 18
# baseline (speedup 1.0000x reference)
import numpy as np
import ml_dtypes
from contextlib import ExitStack

import concourse.bass as bass
import concourse.mybir as mybir
from concourse.bass_utils import run_bass_kernel_spmd
from concourse.alu_op_type import AluOpType

F32 = mybir.dt.float32
BF16 = mybir.dt.bfloat16
AF = mybir.ActivationFunctionType
AX = mybir.AxisListType
OP = AluOpType

S = 2048
D = 2048
HID2 = 8192
HV = 4096
QKD = 128
FF = 5632
OGU = 2 * FF
T = S // 128
KD = D // 128
KF = FF // 128
EPS = 1e-5
C_RNE = float(np.float32(1.5 * 2 ** 23))
INV_S2 = 1.0 / float(S) ** 2

ENGINES = ("tensor", "vector", "scalar", "sync", "gpsimd")


def build_program(debug=False, paranoid=False):
    nc = bass.Bass("TRN2", target_bir_lowering=False, detect_race_conditions=False)

    x_in = nc.dram_tensor("x", [S, D], F32, kind="ExternalInput").ap()
    w_hid = nc.dram_tensor("w_hid", [D, HID2], F32, kind="ExternalInput").ap()
    b_hid = nc.dram_tensor("b_hid", [1, HID2], F32, kind="ExternalInput").ap()
    w_qk = nc.dram_tensor("w_qk", [D, QKD], F32, kind="ExternalInput").ap()
    b_qk = nc.dram_tensor("b_qk", [QKD, 1], F32, kind="ExternalInput").ap()
    g_q = nc.dram_tensor("g_q", [QKD, 1], F32, kind="ExternalInput").ap()
    be_q = nc.dram_tensor("be_q", [QKD, 1], F32, kind="ExternalInput").ap()
    g_k = nc.dram_tensor("g_k", [QKD, 1], F32, kind="ExternalInput").ap()
    be_k = nc.dram_tensor("be_k", [QKD, 1], F32, kind="ExternalInput").ap()
    w_out = nc.dram_tensor("w_out", [HV, D], F32, kind="ExternalInput").ap()
    b_out = nc.dram_tensor("b_out", [1, D], F32, kind="ExternalInput").ap()
    w_gu = nc.dram_tensor("w_gu", [OGU, D], F32, kind="ExternalInput").ap()
    w_dn = nc.dram_tensor("w_dn", [D, FF], F32, kind="ExternalInput").ap()
    nw = nc.dram_tensor("nw", [1, D], F32, kind="ExternalInput").ap()
    ident = nc.dram_tensor("ident", [128, 128], BF16, kind="ExternalInput").ap()
    ones_bf = nc.dram_tensor("ones_bf", [1, 128], BF16, kind="ExternalInput").ap()
    ones_f = nc.dram_tensor("ones_f", [128, 1], F32, kind="ExternalInput").ap()
    ones_rf = nc.dram_tensor("ones_rf", [1, 128], F32, kind="ExternalInput").ap()

    out = nc.dram_tensor("out", [S, D], F32, kind="ExternalOutput").ap()

    def scratch(name, shape, dt):
        if debug:
            return nc.dram_tensor(name, shape, dt, kind="ExternalOutput").ap()
        return nc.dram_tensor(name, shape, dt).ap()

    hid_d = scratch("hid_d", [S, HID2], BF16)
    xqT_d = scratch("xqT_d", [D, S], BF16)
    h2_d = scratch("h2_d", [S, FF], BF16)
    hqT_d = scratch("hqT_d", [T, FF, 128], BF16)
    if debug:
        dbg_ffn = nc.dram_tensor("dbg_ffn", [S, D], F32, kind="ExternalOutput").ap()
        dbg_p1 = nc.dram_tensor("dbg_p1", [128, 8 * T], F32, kind="ExternalOutput").ap()
        dbg_acc = nc.dram_tensor("dbg_acc", [128, T * D], BF16, kind="ExternalOutput").ap()
        dbg_qT = nc.dram_tensor("dbg_qT", [128, S], BF16, kind="ExternalOutput").ap()
        dbg_kT = nc.dram_tensor("dbg_kT", [128, S], BF16, kind="ExternalOutput").ap()
        dbg_st = nc.dram_tensor("dbg_st", [128, 4 * T + 8], F32, kind="ExternalOutput").ap()

    es_p = ExitStack()

    class ESem:
        def __init__(self, name):
            self.h = es_p.enter_context(nc.semaphore(name))
            self.n = 0

    class Sched:
        def __init__(self):
            self.q = {e: [] for e in ENGINES}
            self.psems = None

        def op(self, eng, fn, tick=None, inc=1):
            if self.psems is not None and eng in ("tensor", "vector", "scalar"):
                for oe, sem_ in self.psems.items():
                    if oe != eng and sem_.n > 0:
                        self.wait(eng, sem_, sem_.n)
                if tick is None:
                    tick = self.psems[eng]
                    inc = 1
            cap = None
            if tick is not None:
                tick.n += inc
                cap = (tick.h, inc)

            def thunk(h, fn=fn, cap=cap):
                ins = fn(h)
                if cap is not None:
                    ins.then_inc(cap[0], cap[1])
            self.q[eng].append(thunk)
            return (tick, tick.n) if tick is not None else None

        def wait(self, eng, sem, tgt):
            if tgt <= 0:
                return
            self.q[eng].append(lambda h, s=sem.h, t=tgt: h.wait_ge(s, t))

        def wait_ev(self, eng, ev):
            if ev is not None:
                self.wait(eng, ev[0], ev[1])

    sc = Sched()

    class Ring:
        def __init__(self, nslots):
            self.nslots = nslots
            self.acq = 0
            self.rels = []

        def acquire(self, eng):
            k = self.acq
            self.acq += 1
            if k >= self.nslots:
                idx = k - self.nslots
                assert idx < len(self.rels), "ring release missing (acq %d)" % k
                for ev in self.rels[idx]:
                    sc.wait_ev(eng, ev)
            return k % self.nslots

        def release(self, *evs):
            self.rels.append([e for e in evs if e is not None])

    class DStream:
        def __init__(self, name, nslots, qeng):
            self.load = [ESem(name + "_l%d" % i) for i in range(nslots)]
            self.nslots = nslots
            self.qeng = qeng
            self.issued = 0
            self.rels = []

        def issue(self, pairs):
            k = self.issued
            slot = k % self.nslots
            if k >= self.nslots:
                idx = k - self.nslots
                assert idx < len(self.rels), "dstream release missing (%d)" % k
                for ev in self.rels[idx]:
                    sc.wait_ev(self.qeng, ev)
            for o, i in pairs:
                sc.op(self.qeng, lambda h, o=o, i=i: h.dma_start(o, i),
                      tick=self.load[slot], inc=16)
            self.issued += 1
            return slot, self.load[slot].n

        def release(self, *evs):
            self.rels.append([e for e in evs if e is not None])

    class UStore:
        def __init__(self, name, nslots, qeng="sync"):
            self.st = [ESem(name + "_s%d" % i) for i in range(nslots)]
            self.nslots = nslots
            self.qeng = qeng
            self.n = 0

        def acquire(self, eng):
            k = self.n
            slot = k % self.nslots
            if k >= self.nslots:
                sc.wait(eng, self.st[slot], self.st[slot].n)
            return slot

        def issue(self, dst, src):
            slot = self.n % self.nslots
            sc.op(self.qeng, lambda h, d=dst, s=src: h.dma_start(d, s),
                  tick=self.st[slot], inc=16)
            self.n += 1

        def wait_all(self, eng):
            for s_ in self.st:
                sc.wait(eng, s_, s_.n)

    s_pe = ESem("s_pe")
    s_vec = ESem("s_vec")
    s_act = ESem("s_act")
    s_win = [ESem("s_win%d" % i) for i in range(4)]
    if paranoid:
        sc.psems = {"tensor": s_pe, "vector": s_vec, "scalar": s_act}

    def fence(*engs):
        tg = {"tensor": s_pe.n, "vector": s_vec.n, "scalar": s_act.n}
        for e in engs:
            for oe, sem_ in (("tensor", s_pe), ("vector", s_vec), ("scalar", s_act)):
                if oe != e and tg[oe] > 0:
                    sc.wait(e, sem_, tg[oe])

    sbp = lambda nm, shape, dt: es_p.enter_context(nc.sbuf_tensor(nm, shape, dt))
    qT = sbp("qT", [128, S], BF16)
    kT = sbp("kT", [128, S], BF16)
    id_sb = sbp("id_sb", [128, 128], BF16)
    onesb_sb = sbp("onesb_sb", [1, 128], BF16)
    onesf_sb = sbp("onesf_sb", [128, 1], F32)
    onesrf_sb = sbp("onesrf_sb", [1, 128], F32)
    bqk_sb = sbp("bqk_sb", [QKD, 1], F32)
    gq_sb = sbp("gq_sb", [QKD, 1], F32)
    bq_sb = sbp("bq_sb", [QKD, 1], F32)
    gk_sb = sbp("gk_sb", [QKD, 1], F32)
    bk_sb = sbp("bk_sb", [QKD, 1], F32)
    rinv1_all = sbp("rinv1_all", [128, T], F32)
    negmr_all = sbp("negmr_all", [128, T], F32)
    rs1_all = sbp("rs1_all", [128, T], F32)
    rs2_all = sbp("rs2_all", [128, T], F32)
    c_rne_t = sbp("c_rne_t", [128, 1], F32)
    invs2_t = sbp("invs2_t", [128, 1], F32)
    spacer_t = sbp("spacer_t", [128, 1024], F32)
    abscols = sbp("abscols", [128, 288], F32)
    thr_gu = sbp("thr_gu", [128, 1], F32)
    nthr_gu = sbp("nthr_gu", [128, 1], F32)
    ncm_gu = sbp("ncm_gu", [128, 1], F32)
    thr_dn = sbp("thr_dn", [128, 1], F32)
    nthr_dn = sbp("nthr_dn", [128, 1], F32)
    ncm_dn = sbp("ncm_dn", [128, 1], F32)
    red_tmp = sbp("red_tmp", [128, 2], F32)
    cm_row = sbp("cm_row", [1, 2], F32)
    cmb = sbp("cmb", [128, 2], F32)
    sg_col = sbp("sg_col", [128, 1], F32)
    sd_col = sbp("sd_col", [128, 1], F32)
    sg_all = sbp("sg_all", [128, T], F32)
    sd_all = sbp("sd_all", [128, T], F32)
    trashw = sbp("trashw", [128, 1024], BF16)
    sf2_all = sbp("sf2_all", [128, T], F32)
    zero_bf = sbp("zero_bf", [128, 128], BF16)
    bnst_t = sbp("bnst_t", [128, 24], F32)
    dbg1_sb = sbp("dbg1_sb", [128, 8 * T], F32) if debug else None
    bnag_t = sbp("bnag_t", [128, 2], F32)
    stn = ("sum", "sq", "m", "msq", "m2", "varp", "vpe", "rinv2", "rinv",
           "negmr", "msv", "mse", "r2a", "rmsinv", "mx", "cmx", "icx", "sfac",
           "mx2", "cmx2", "icx2", "sfac2")
    st = {nm: sbp("st_" + nm, [128, 1], F32) for nm in stn}

    pf = [es_p.enter_context(nc.psum_tensor("pf%d" % i, [128, 512], F32)) for i in range(6)]
    pb = [es_p.enter_context(nc.psum_tensor("pb%d" % i, [128, 1024], BF16)) for i in range(2)]

    r_tp = Ring(2)

    def hbm_pkc(ap2d, a0, a1, c0, c1):
        return ap2d.rearrange("(a p) c -> p a c", p=128)[:, a0:a1, c0:c1]

    def sb_pkc(t, k, k0, k1, c0, c1):
        return t.rearrange("p (k c) -> p k c", k=k)[:, k0:k1, c0:c1]

    ld0 = DStream("c0", 1, "gpsimd")
    _, n0 = ld0.issue([
        (id_sb[:], ident[:, :]), (onesb_sb[:], ones_bf[:, :]),
        (onesf_sb[:], ones_f[:, :]), (onesrf_sb[:], ones_rf[:, :]),
        (bqk_sb[:], b_qk[:, :]), (gq_sb[:], g_q[:, :]), (bq_sb[:], be_q[:, :]),
        (gk_sb[:], g_k[:, :]), (bk_sb[:], be_k[:, :])])
    s_mem0 = ESem("s_mem0")
    if debug:
        sc.op("gpsimd", lambda h: h.memset(dbg1_sb[:], 0.0))
    sc.op("gpsimd", lambda h: h.memset(zero_bf[:], 0.0))
    sc.op("gpsimd", lambda h: h.memset(c_rne_t[:], C_RNE))
    sc.op("gpsimd", lambda h: h.memset(invs2_t[:], INV_S2))
    sc.op("gpsimd", lambda h: h.memset(spacer_t[:], 1.0), tick=s_mem0)

    es_r1 = ExitStack()
    xnT = es_r1.enter_context(nc.sbuf_tensor("xnT", [128, KD * S], BF16))

    es_a = ExitStack()
    sba = lambda nm, shape, dt: es_a.enter_context(nc.sbuf_tensor(nm, shape, dt))
    g_bcast = sba("g_bcast", [128, D], F32)
    nw_row = sba("nw_row", [1, D], F32)
    x_sb = [sba("x_sb%d" % i, [128, D], F32) for i in range(3)]
    bnag2 = [sba("bnag2_%d" % i, [128, 2], F32) for i in range(2)]
    rinv2_2 = [sba("rinv2_2_%d" % i, [128, 1], F32) for i in range(2)]
    rinv_2 = [sba("rinv_2_%d" % i, [128, 1], F32) for i in range(2)]
    rmsinv_2 = [sba("rmsinv_2_%d" % i, [128, 1], F32) for i in range(2)]
    sfac_2 = [sba("sfac_2_%d" % i, [128, 1], F32) for i in range(2)]
    xn_sb = [sba("xn_sb%d" % i, [128, D], BF16) for i in range(2)]
    xnorm_sb = [sba("xnorm%d" % i, [128, D], F32) for i in range(2)]
    y_sb = [sba("y_sb%d" % i, [128, D], F32) for i in range(2)]
    xq_sb = [sba("xq_sb%d" % i, [128, D], BF16) for i in range(2)]
    xqst_sb = [sba("xqst%d" % i, [128, D], BF16) for i in range(2)]
    trash1 = sba("trash1", [128, D], BF16)
    trash2 = sba("trash2", [128, D], BF16)

    ld_x = DStream("ldx", 3, "gpsimd")
    ld_nw = DStream("ldnw", 1, "gpsimd")
    r_xnorm = Ring(2)
    r_y = Ring(2)
    r_xn = Ring(2)
    us_xq = UStore("us_xq", 2)

    _, nnw = ld_nw.issue([(nw_row[:], nw[:, :])])
    sc.wait("tensor", ld_nw.load[0], nnw)
    sc.wait("tensor", ld0.load[0], n0)
    r_gb = Ring(2)
    for j in range(D // 512):
        slot = r_gb.acquire("tensor")
        ev = sc.op("tensor", lambda h, s=slot, j=j: h.matmul(
            pf[s][:], onesrf_sb[:], nw_row[:, 512 * j:512 * (j + 1)],
            start=True, stop=True), tick=s_pe)
        sc.wait_ev("vector", ev)
        ev_c = sc.op("vector", lambda h, s=slot, j=j: h.tensor_copy(
            g_bcast[:, 512 * j:512 * (j + 1)], pf[s][:]), tick=s_vec)
        r_gb.release(ev_c)
    v_gb = s_vec.n

    V = lambda fn, **kw: sc.op("vector", fn, **kw)
    SP = lambda **kw: sc.op("vector", lambda h: h.tensor_copy(
        spacer_t[:, 0:512], spacer_t[:, 512:1024]), **kw)
    p1 = {}

    def p1_S1a(t):
        s = t % 2
        xslot, lx = ld_x.issue([(x_sb[t % 3][:], x_in[128 * t:128 * (t + 1), :])])
        p1[("xs", t)] = xslot
        sc.wait("vector", ld_x.load[xslot], lx)
        if t == 0:
            sc.wait("vector", s_vec, v_gb)
        for j in range(4):
            V(lambda h, xs=xslot, j=j: h.bn_stats(
                bnst_t[:, 6 * j:6 * (j + 1)], x_sb[xs][:, 512 * j:512 * (j + 1)]))
        SP()
        V(lambda h, s=s: h.bn_aggr(bnag2[s][:], bnst_t[:]))
        SP()
        V(lambda h, s=s: h.tensor_scalar(st["vpe"][:], bnag2[s][:, 1:2], EPS, None, OP.add))
        SP()
        V(lambda h, s=s: h.reciprocal(rinv2_2[s][:], st["vpe"][:]))
        ev_r2 = SP(tick=s_vec)
        sc.wait_ev("scalar", ev_r2)
        p1[("ri", t)] = sc.op("scalar", lambda h, s=s: h.activation(
            rinv_2[s][:], rinv2_2[s][:], AF.Sqrt), tick=s_act)

    def p1_S1b(t):
        s = t % 2
        xslot = p1[("xs", t)]
        sc.wait_ev("vector", p1[("ri", t)])
        if debug:
            for di, src_ap in enumerate((bnag2[s][:, 1:2], rinv2_2[s][:], rinv_2[s][:])):
                V(lambda h, t=t, di=di, a=src_ap: h.tensor_copy(
                    dbg1_sb[:, 8 * t + di:8 * t + di + 1], a))
        V(lambda h, s=s, t=t: h.tensor_scalar(rinv1_all[:, t:t + 1], rinv_2[s][:], 1.0, None, OP.add))
        ev_nm = V(lambda h, s=s, t=t: h.tensor_scalar(
            negmr_all[:, t:t + 1], bnag2[s][:, 0:1], rinv_2[s][:], -1.0, OP.mult, OP.mult),
            tick=s_vec)
        V(lambda h, s=s: h.scalar_tensor_tensor(st["msv"][:], bnag2[s][:, 1:2], 1.0, rinv2_2[s][:], OP.mult, OP.mult))
        SP()
        V(lambda h: h.tensor_scalar(st["mse"][:], st["msv"][:], EPS, None, OP.add))
        SP()
        V(lambda h: h.reciprocal(st["r2a"][:], st["mse"][:]))
        ev_ra = SP(tick=s_vec)
        sc.wait_ev("scalar", ev_ra)
        p1[("rms", t)] = sc.op("scalar", lambda h, s=s: h.activation(
            rmsinv_2[s][:], st["r2a"][:], AF.Sqrt), tick=s_act)
        xnslot = r_xn.acquire("scalar")
        p1[("xn_s", t)] = xnslot
        sc.wait_ev("scalar", ev_nm)
        ev_xn = sc.op("scalar", lambda h, xs=xslot, ns=xnslot, s=s, t=t: h.activation(
            xn_sb[ns][:], x_sb[xs][:], AF.Identity, bias=negmr_all[:, t:t + 1], scale=rinv_2[s][:]),
            tick=s_act)
        p1[("xn", t)] = ev_xn
        ld_x.release(ev_xn)

    def p1_S2a(t):
        s = t % 2
        xnslot = p1[("xn_s", t)]
        sc.wait_ev("tensor", p1[("xn", t)])
        evs_tp = []
        for half in range(2):
            ps_ = r_tp.acquire("tensor")
            ev_tp = None
            for j in range(8):
                jj = 8 * half + j
                tk = s_pe if j == 7 else None
                e = sc.op("tensor", lambda h, ns=xnslot, ps=ps_, j=j, jj=jj: h.transpose(
                    pb[ps][:, 128 * j:128 * (j + 1)], xn_sb[ns][:, 128 * jj:128 * (jj + 1)],
                    id_sb[:]), tick=tk)
                ev_tp = e or ev_tp
            sc.wait_ev("scalar", ev_tp)
            dst = sb_pkc(xnT, KD, 8 * half, 8 * half + 8, 128 * t, 128 * (t + 1))
            src = pb[ps_].rearrange("p (k c) -> p k c", k=8)
            ev_ev = sc.op("scalar", lambda h, d=dst, s2=src: h.activation(
                d, s2, AF.Identity), tick=s_act)
            r_tp.release(ev_ev)
            evs_tp.append(ev_tp)

        nslot = r_xnorm.acquire("vector")
        sc.wait_ev("vector", p1[("rms", t)])
        V(lambda h, ns=xnslot, s2=nslot, s=s: h.scalar_tensor_tensor(
            xnorm_sb[s2][:], xn_sb[ns][:], rmsinv_2[s][:], g_bcast[:], OP.mult, OP.mult))
        V(lambda h, s2=nslot: h.tensor_reduce(
            st["mx"][:], xnorm_sb[s2][:], AX.X, OP.max, apply_absolute_value=True))
        SP()
        V(lambda h: h.tensor_scalar(st["cmx"][:], st["mx"][:], 1e-5, None, OP.max))
        SP()
        V(lambda h, t=t: h.tensor_scalar(rs1_all[:, t:t + 1], st["cmx"][:], 1.0 / 127, None, OP.mult))
        SP()
        V(lambda h: h.reciprocal(st["icx"][:], st["cmx"][:]))
        SP()
        V(lambda h, s=s: h.tensor_scalar(sfac_2[s][:], st["icx"][:], 127.0, None, OP.mult))
        ev_sf = SP(tick=s_vec)
        r_xn.release(ev_sf, evs_tp[1])
        yslot = r_y.acquire("scalar")
        p1[("y_s", t)] = yslot
        sc.wait_ev("scalar", ev_sf)
        if t == 0:
            sc.wait("scalar", s_mem0, 1)
        p1[("y", t)] = sc.op("scalar", lambda h, s2=nslot, ys=yslot, s=s: h.activation(
            y_sb[ys][:], xnorm_sb[s2][:], AF.Identity, bias=c_rne_t[:], scale=sfac_2[s][:]),
            tick=s_act)
        r_xnorm.release(p1[("y", t)])

    def p1_S2b(t):
        sq = t % 2
        yslot = p1[("y_s", t)]
        sc.wait_ev("vector", p1[("y", t)])
        ev_xq = V(lambda h, ys=yslot, sq=sq: h.tensor_scalar(
            xq_sb[sq][:], y_sb[ys][:], C_RNE, None, OP.subtract), tick=s_vec)
        r_y.release(ev_xq)

        stslot = us_xq.acquire("scalar")
        sc.wait_ev("tensor", ev_xq)
        for half in range(2):
            ps_ = r_tp.acquire("tensor")
            ev_tp = None
            for j in range(8):
                jj = 8 * half + j
                tk = s_pe if j == 7 else None
                e = sc.op("tensor", lambda h, sq=sq, ps=ps_, j=j, jj=jj: h.transpose(
                    pb[ps][:, 128 * j:128 * (j + 1)], xq_sb[sq][:, 128 * jj:128 * (jj + 1)],
                    id_sb[:]), tick=tk)
                ev_tp = e or ev_tp
            sc.wait_ev("scalar", ev_tp)
            ev_ev = sc.op("scalar", lambda h, ps=ps_, ss=stslot, hf=half: h.activation(
                xqst_sb[ss][:, 1024 * hf:1024 * (hf + 1)], pb[ps][:], AF.Identity), tick=s_act)
            r_tp.release(ev_ev)
        sc.wait("sync", s_act, s_act.n)
        dstq = hbm_pkc(xqT_d, 0, KD, 128 * t, 128 * (t + 1))
        srcq = sb_pkc(xqst_sb[stslot], KD, 0, KD, 0, 128)
        us_xq.issue(dstq, srcq)

    p1_S1a(0)
    p1_S1a(1)
    for t in range(T):
        p1_S1b(t)
        if t + 2 < T:
            p1_S1a(t + 2)
        if t > 0:
            p1_S2b(t - 1)
        p1_S2a(t)
    p1_S2b(T - 1)
    v_p1 = s_vec.n

    es_a.close()

    es_b = ExitStack()
    sbb = lambda nm, shape, dt: es_b.enter_context(nc.sbuf_tensor(nm, shape, dt))
    whb = [sbb("whb%d" % i, [128, KD * 512], BF16) for i in range(2)]
    hstg = sbb("hstg0", [128, T * 512], BF16)
    qk_bf = sbb("qk_bf", [128, 512], BF16)
    wqk_b = sbb("wqk_b", [128, KD * 128], BF16)
    bhid_row = sbb("bhid_row", [1, HID2], BF16)

    fence("gpsimd", "sync", "scalar")
    us_xq.wait_all("gpsimd")
    us_xq.wait_all("scalar")

    ld_bh = DStream("ldbh", 1, "gpsimd")
    ld_qk = DStream("ldqk", 1, "gpsimd")
    ld_wh = DStream("ldwh", 2, "gpsimd")
    us_hid = UStore("us_hid", 1)
    r_hps = Ring(4)
    r_qkps = Ring(2)

    _, nbh = ld_bh.issue([(bhid_row[:], b_hid[:, :])])
    sc.wait("tensor", ld_bh.load[0], nbh)

    _, nqk = ld_qk.issue([(wqk_b[:, 128 * k:128 * (k + 1)],
                           w_qk[128 * k:128 * (k + 1), :]) for k in range(KD)])
    sc.wait("tensor", ld_qk.load[0], nqk)
    sc.wait("tensor", s_vec, v_p1)
    for strip in range(4):
        pslot = r_qkps.acquire("tensor")
        ev_mm = None
        for k in range(KD):
            tk = s_pe if k == KD - 1 else None
            e = sc.op("tensor", lambda h, ps=pslot, k=k, sp=strip: h.matmul(
                pf[4 + ps][:], wqk_b[:, 128 * k:128 * (k + 1)],
                xnT[:, S * k + 512 * sp: S * k + 512 * (sp + 1)],
                start=(k == 0), stop=(k == KD - 1)), tick=tk)
            ev_mm = e or ev_mm
        sc.wait_ev("scalar", ev_mm)
        ev_si = sc.op("scalar", lambda h, ps=pslot: h.activation(
            qk_bf[:], pf[4 + ps][:], AF.Silu, bias=bqk_sb[:]), tick=s_act)
        r_qkps.release(ev_si)
        sc.wait_ev("vector", ev_si)
        sc.op("vector", lambda h, sp=strip: h.tensor_scalar(
            qT[:, 512 * sp:512 * (sp + 1)], qk_bf[:], gq_sb[:], bq_sb[:], OP.mult, OP.add))
        sc.op("vector", lambda h, sp=strip: h.tensor_scalar(
            kT[:, 512 * sp:512 * (sp + 1)], qk_bf[:], gk_sb[:], bk_sb[:], OP.mult, OP.add),
            tick=s_vec)
    v_qk = s_vec.n

    mean_jobs = []
    for c in range(OGU // 128):
        for hh in range(2):
            mean_jobs.append((w_gu[128 * c:128 * (c + 1), 1024 * hh:1024 * (hh + 1)], 1024))
    NJ_GU = len(mean_jobs)
    for rb in range(KD):
        for c0 in range(0, FF, 1024):
            w = min(1024, FF - c0)
            mean_jobs.append((w_dn[128 * rb:128 * (rb + 1), c0:c0 + w], w))
    NMJ = len(mean_jobs)
    m_iss = [0]
    m_con = [0]

    def mean_issue_one():
        i = m_iss[0]
        src, w = mean_jobs[i]
        sc.op("scalar", lambda h, s=i % 3, src=src, w=w: h.dma_start(
            wmean_b[s][:, :w], src), tick=s_win[i % 3], inc=16)
        m_iss[0] += 1

    def mean_pump(k):
        for _ in range(k):
            if m_con[0] >= NMJ:
                return
            while m_iss[0] < NMJ and m_iss[0] < m_con[0] + 3:
                mean_issue_one()
            i = m_con[0]
            w = mean_jobs[i][1]
            sc.wait("scalar", s_win[i % 3], 16 * (i // 3 + 1))
            sc.op("scalar", lambda h, s=i % 3, w=w, i=i: h.activation(
                trashw[:, :w], wmean_b[s][:, :w], AF.Abs, accum_out=abscols[:, i:i + 1]),
                tick=s_act)
            m_con[0] += 1

    for strip in range(HID2 // 512):
        wnext = ld_wh.issued % 2
        wslot, lw = ld_wh.issue([(whb[wnext][:, 512 * k:512 * (k + 1)],
                                  w_hid[128 * k:128 * (k + 1), 512 * strip:512 * (strip + 1)])
                                 for k in range(KD)])
        us_hid.acquire("scalar")
        sc.wait("tensor", ld_wh.load[wslot], lw)
        for t in range(T):
            pslot = r_hps.acquire("tensor")
            sc.op("tensor", lambda h, ps=pslot, sp=strip: h.matmul(
                pf[ps][:], onesb_sb[:], bhid_row[:, 512 * sp:512 * (sp + 1)],
                start=True, stop=False))
            ev_mm = None
            for k in range(KD):
                tk = s_pe if k == KD - 1 else None
                e = sc.op("tensor", lambda h, ps=pslot, k=k, t=t, ws=wslot: h.matmul(
                    pf[ps][:], xnT[:, S * k + 128 * t: S * k + 128 * (t + 1)],
                    whb[ws][:, 512 * k:512 * (k + 1)],
                    start=False, stop=(k == KD - 1)), tick=tk)
                ev_mm = e or ev_mm
            if t == T - 1:
                ld_wh.release(ev_mm)
            sc.wait_ev("scalar", ev_mm)
            ev_si = sc.op("scalar", lambda h, ps=pslot, t=t: h.activation(
                hstg[:, 512 * t:512 * (t + 1)], pf[ps][:], AF.Silu), tick=s_act)
            r_hps.release(ev_si)
        sc.wait("sync", s_act, s_act.n)
        dsth = hbm_pkc(hid_d, 0, T, 512 * strip, 512 * (strip + 1))
        srch = sb_pkc(hstg, T, 0, T, 0, 512)
        us_hid.issue(dsth, srch)
    a_p2 = s_act.n

    es_b.close()
    es_r1.close()

    es_r2 = ExitStack()
    acc = es_r2.enter_context(nc.sbuf_tensor("acc", [128, T * D], BF16))

    es_c = ExitStack()
    sbc = lambda nm, shape, dt: es_c.enter_context(nc.sbuf_tensor(nm, shape, dt))
    bout_row = sbc("bout_row", [1, D], BF16)
    vq_sb = [sbc("vq_sb%d" % i, [128, T * 1024], BF16) for i in range(2)]
    gq_t = [sbc("gq_t%d" % i, [128, 1024], BF16) for i in range(2)]
    woq_sb = sbc("woq_sb", [128, 8 * D], BF16)
    wmean_b = [sbc("wmean%d" % i, [128, 1024], F32) for i in range(3)]
    a_stage = sbc("a_stage", [128, 1024], BF16)
    a_tmp = [sbc("a_tmp%d" % i, [128, 512], BF16) for i in range(2)]
    avg_stage = sbc("avg_stage", [128, 1024], BF16)
    avgT_stage = sbc("avgT_stage", [128, 1024], BF16)

    fence("gpsimd", "sync", "vector")
    us_hid.wait_all("vector")

    ld_bo = DStream("ldbo", 1, "gpsimd")
    ld_v = DStream("ldv", 2, "gpsimd")
    ld_g = DStream("ldg", 2, "gpsimd")
    ld_wo = DStream("ldwo", 2, "gpsimd")
    r_sim = Ring(2)
    r_av = Ring(1)
    r_out = Ring(1)
    r_ast = Ring(2)
    r_atmp = Ring(2)
    r_avgs = Ring(1)
    r_avT = Ring(1)

    _, nbo = ld_bo.issue([(bout_row[:], b_out[:, :])])
    sc.wait("tensor", ld_bo.load[0], nbo)
    sc.wait("tensor", s_act, a_p2)

    us_hid.wait_all("gpsimd")
    ev_wo_last = None
    vslot0, lv0 = ld_v.issue([(vq_sb[0][:, 1024 * t:1024 * (t + 1)],
                               hid_d[128 * t:128 * (t + 1), 0:1024]) for t in range(T)])
    vload = {0: (vslot0, lv0)}
    for hp in range(4):
        vslot, lv = vload[hp]
        lwo_h = []
        for dh in range(2):
            _, lw_ = ld_wo.issue([(woq_sb[:, D * k + 1024 * dh:D * k + 1024 * (dh + 1)],
                                   w_out[1024 * hp + 128 * k:1024 * hp + 128 * (k + 1),
                                         1024 * dh:1024 * (dh + 1)])
                                  for k in range(8)])
            lwo_h.append(lw_)
        if hp + 1 < 4:
            vload[hp + 1] = ld_v.issue(
                [(vq_sb[(hp + 1) % 2][:, 1024 * t:1024 * (t + 1)],
                  hid_d[128 * t:128 * (t + 1), 1024 * (hp + 1):1024 * (hp + 2)])
                 for t in range(T)])
        if hp == 0:
            sc.wait("tensor", s_vec, v_qk)
        sim_evs = {}
        sim_slots = {}

        def emit_sims(t, g, hp=hp, vslot=vslot, lv=lv):
            pslot = r_sim.acquire("tensor")
            if t == 0 and g == 0:
                sc.wait("tensor", ld_v.load[vslot], lv)
            ev_sim = None
            for j in range(4):
                kt = 4 * g + j
                tk = s_pe if j == 3 else None
                e = sc.op("tensor", lambda h, ps=pslot, kt=kt, j=j, t=t: h.matmul(
                    pf[ps][:, 128 * j:128 * (j + 1)],
                    kT[:, 128 * kt:128 * (kt + 1)],
                    qT[:, 128 * t:128 * (t + 1)],
                    start=True, stop=True), tick=tk)
                ev_sim = e or ev_sim
            sim_evs[(t, g)] = ev_sim
            sim_slots[(t, g)] = pslot

        emit_sims(0, 0)
        emit_sims(0, 1)
        for t in range(T):
            gslot, lg = ld_g.issue([(gq_t[ld_g.issued % 2][:],
                                     hid_d[128 * t:128 * (t + 1),
                                           HV + 1024 * hp:HV + 1024 * (hp + 1)])])
            r_av.acquire("tensor")
            ev_av = None
            for g in range(4):
                pslot = sim_slots.pop((t, g))
                aslot = r_ast.acquire("vector")
                atslot = r_atmp.acquire("scalar")
                sc.wait_ev("scalar", sim_evs.pop((t, g)))
                ev_relu = sc.op("scalar", lambda h, ps=pslot, ats=atslot: h.activation(
                    a_tmp[ats][:], pf[ps][:], AF.Relu), tick=s_act)
                r_sim.release(ev_relu)
                sc.wait_ev("vector", ev_relu)
                ev_a = sc.op("vector", lambda h, ats=atslot, a=aslot: h.tensor_tensor(
                    a_stage[:, 512 * a:512 * (a + 1)], a_tmp[ats][:], a_tmp[ats][:],
                    OP.mult), tick=s_vec)
                r_atmp.release(ev_a)
                if g + 2 < 4:
                    emit_sims(t, g + 2)
                elif t + 1 < T:
                    emit_sims(t + 1, g - 2)
                sc.wait_ev("tensor", ev_a)
                for j in range(4):
                    kt = 4 * g + j
                    for hf in range(2):
                        tk = s_pe if (j == 3 and hf == 1) else None
                        e = sc.op("tensor", lambda h, j=j, kt=kt, hf=hf, a=aslot, vs=vslot: h.matmul(
                            pf[2 + hf][:],
                            a_stage[:, 512 * a + 128 * j:512 * a + 128 * (j + 1)],
                            vq_sb[vs][:, 1024 * kt + 512 * hf:1024 * kt + 512 * (hf + 1)],
                            start=(kt == 0), stop=(kt == T - 1)), tick=tk)
                        ev_av = e or ev_av
                r_ast.release(ev_av)
            mean_pump(5)
            r_avgs.acquire("vector")
            sc.wait_ev("vector", ev_av)
            sc.wait("vector", ld_g.load[gslot], lg)
            if hp == 0 and t == 0:
                sc.wait("vector", s_mem0, 1)
            r_avT.acquire("tensor")
            ev_aT_h = []
            ev_avg = None
            for hf in range(2):
                e_avg = sc.op("vector", lambda h, hf=hf, gs=gslot: h.scalar_tensor_tensor(
                    avg_stage[:, 512 * hf:512 * (hf + 1)], pf[2 + hf][:], invs2_t[:],
                    gq_t[gs][:, 512 * hf:512 * (hf + 1)],
                    OP.mult, OP.mult), tick=s_vec)
                ev_avg = e_avg
                ps_h = r_tp.acquire("tensor")
                sc.wait_ev("tensor", e_avg)
                ev_tp = None
                for j in range(4):
                    jj = 4 * hf + j
                    tk = s_pe if j == 3 else None
                    e = sc.op("tensor", lambda h, ps=ps_h, j=j, jj=jj: h.transpose(
                        pb[ps][:, 128 * j:128 * (j + 1)], avg_stage[:, 128 * jj:128 * (jj + 1)],
                        id_sb[:]), tick=tk)
                    ev_tp = e or ev_tp
                if hf == 1:
                    r_avgs.release(ev_tp)
                sc.wait_ev("vector", ev_tp)
                e_aT = sc.op("vector", lambda h, ps=ps_h, hf=hf: h.tensor_copy(
                    avgT_stage[:, 512 * hf:512 * (hf + 1)], pb[ps][:, 0:512]), tick=s_vec)
                ev_aT_h.append(e_aT)
                r_tp.release(e_aT)
            r_av.release(ev_avg)
            ld_g.release(ev_avg)
            for dh in range(2):
                sc.wait("tensor", ld_wo.load[dh], lwo_h[dh])
                r_out.acquire("tensor")
                ev_wo = None
                if hp == 0:
                    for j2 in range(2):
                        sc.op("tensor", lambda h, dh=dh, j2=j2: h.matmul(
                            pf[4 + j2][:], onesb_sb[:],
                            bout_row[:, 1024 * dh + 512 * j2:1024 * dh + 512 * (j2 + 1)],
                            start=True, stop=False))
                for hc in range(8):
                    if dh == 0 and hc == 0:
                        sc.wait_ev("tensor", ev_aT_h[0])
                    elif dh == 0 and hc == 4:
                        sc.wait_ev("tensor", ev_aT_h[1])
                    for j2 in range(2):
                        tk = s_pe if (hc == 7 and j2 == 1) else None
                        e = sc.op("tensor", lambda h, dh=dh, hc=hc, j2=j2, hp=hp: h.matmul(
                            pf[4 + j2][:],
                            avgT_stage[:, 128 * hc:128 * (hc + 1)],
                            woq_sb[:, D * hc + 1024 * dh + 512 * j2:
                                   D * hc + 1024 * dh + 512 * (j2 + 1)],
                            start=(hc == 0 and hp != 0), stop=(hc == 7)), tick=tk)
                        ev_wo = e or ev_wo
                if dh == 1:
                    r_avT.release(ev_wo)
                if t == T - 1:
                    ld_wo.release(ev_wo)
                    if dh == 1 and hp == 3:
                        ev_wo_last = ev_wo
                sc.wait_ev("vector", ev_wo)
                ev_ac = None
                for j2 in range(2):
                    dsta = acc[:, D * t + 1024 * dh + 512 * j2: D * t + 1024 * dh + 512 * (j2 + 1)]
                    tk = s_vec if j2 == 1 else None
                    if hp == 0:
                        e = sc.op("vector", lambda h, d=dsta, j2=j2: h.tensor_copy(
                            d, pf[4 + j2][:]), tick=tk)
                    else:
                        e = sc.op("vector", lambda h, d=dsta, j2=j2: h.tensor_tensor(
                            d, pf[4 + j2][:], d, OP.add), tick=tk)
                    ev_ac = e or ev_ac
                r_out.release(ev_ac)
            if t == T - 1:
                ld_v.release(ev_av)
    v_p3 = s_vec.n

    es_c.close()

    mean_pump(NMJ)
    a_means = s_act.n
    sc.wait("vector", s_act, a_means)
    V2 = lambda fn, **kw: sc.op("vector", fn, **kw)
    SP2 = lambda **kw: sc.op("vector", lambda h: h.tensor_copy(
        spacer_t[:, 0:512], spacer_t[:, 512:1024]), **kw)
    V2(lambda h: h.tensor_reduce(red_tmp[:, 0:1], abscols[:, 0:NJ_GU], AX.X, OP.add))
    V2(lambda h, a=NJ_GU, b=len(mean_jobs): h.tensor_reduce(
        red_tmp[:, 1:2], abscols[:, a:b], AX.X, OP.add))
    ev_red = SP2(tick=s_vec)
    sc.wait_ev("tensor", ev_red)
    sc.wait_ev("tensor", ev_wo_last)
    sc.wait("tensor", s_vec, v_p3)
    sc.op("tensor", lambda h: h.matmul(pf[0][0:1, 0:1], red_tmp[:, 0:1], onesf_sb[:],
                                       start=True, stop=True))
    ev_ps = sc.op("tensor", lambda h: h.matmul(pf[0][0:1, 1:2], red_tmp[:, 1:2], onesf_sb[:],
                                               start=True, stop=True), tick=s_pe)
    sc.wait_ev("vector", ev_ps)
    V2(lambda h: h.tensor_copy(cm_row[:, :], pf[0][0:1, 0:2]))
    SP2()
    V2(lambda h: h.tensor_scalar(cm_row[:, 0:1], cm_row[:, 0:1], 1.0 / (OGU * D), None, OP.mult))
    V2(lambda h: h.tensor_scalar(cm_row[:, 1:2], cm_row[:, 1:2], 1.0 / (D * FF), None, OP.mult))
    SP2()
    V2(lambda h: h.tensor_scalar(cm_row[:, :], cm_row[:, :], 1e-5, None, OP.max))
    ev_cm = SP2(tick=s_vec)
    sc.wait_ev("tensor", ev_cm)
    ev_bc = sc.op("tensor", lambda h: h.matmul(pf[1][:, 0:2], onesrf_sb[:], cm_row[:, :],
                                               start=True, stop=True), tick=s_pe)
    sc.wait_ev("vector", ev_bc)
    V2(lambda h: h.tensor_copy(cmb[:, :], pf[1][:, 0:2]))
    SP2()
    V2(lambda h: h.tensor_scalar(thr_gu[:], cmb[:, 0:1], 0.5, None, OP.mult))
    V2(lambda h: h.tensor_scalar(nthr_gu[:], cmb[:, 0:1], -0.5, None, OP.mult))
    V2(lambda h: h.tensor_scalar(ncm_gu[:], cmb[:, 0:1], -1.0, None, OP.mult))
    V2(lambda h: h.tensor_scalar(thr_dn[:], cmb[:, 1:2], 0.5, None, OP.mult))
    V2(lambda h: h.tensor_scalar(nthr_dn[:], cmb[:, 1:2], -0.5, None, OP.mult))
    ev_thr = V2(lambda h: h.tensor_scalar(ncm_dn[:], cmb[:, 1:2], -1.0, None, OP.mult),
                tick=s_vec)

    es_d = ExitStack()
    sbd = lambda nm, shape, dt: es_d.enter_context(nc.sbuf_tensor(nm, shape, dt))
    xqT_sb = sbd("xqT_sb", [128, KD * S], BF16)
    wqT_g = [sbd("wqT_g%d" % i, [128, KD * 256], BF16) for i in range(2)]
    wqT_u = [sbd("wqT_u%d" % i, [128, KD * 256], BF16) for i in range(2)]
    wg_in = [sbd("wg_in%d" % i, [128, D], F32) for i in range(1)]
    c1_bf = sbd("c1_bf", [128, D], BF16)
    wqn_bf = [sbd("wqn%d" % i, [128, D], BF16) for i in range(2)]
    g_blk = sbd("g_blk", [128, 256], BF16)
    gup_blk = sbd("gup_blk", [128, 256], BF16)
    h2stg = sbd("h2stg", [128, T * 256], BF16)

    fence("gpsimd", "sync")

    ld_xqT = DStream("ldxqT", 1, "gpsimd")
    ld_wg = DStream("ldwg", 1, "gpsimd")
    us_h2 = UStore("us_h2", 1)
    r_wqn = Ring(2)
    r_gups = Ring(2)
    r_gblk = Ring(1)

    sc.wait_ev("vector", ev_thr)
    sc.op("vector", lambda h: h.tensor_copy(spacer_t[:, 0:512], spacer_t[:, 512:1024]))
    ev_sg = sc.op("vector", lambda h: h.tensor_scalar(
        sg_all[:], rs1_all[:], ncm_gu[:], None, OP.mult), tick=s_vec)
    us_xq.wait_all("gpsimd")
    _, lxq = ld_xqT.issue([(sb_pkc(xqT_sb, KD, 0, KD, 0, S),
                            hbm_pkc(xqT_d, 0, KD, 0, S))])
    sc.wait("tensor", ld_xqT.load[0], lxq)
    sc.wait_ev("tensor", ev_thr)

    NP = OGU // 512
    r_wqT = Ring(2)
    build_done = {}
    pend_q = {}

    def emit_build_quant(jp, which, c, first=False):
        base = 256 * jp + (FF if which else 0)
        wslot, lwg = ld_wg.issue([(wg_in[0][:],
                                   w_gu[base + 128 * c: base + 128 * (c + 1), :])])
        qslot = r_wqn.acquire("vector")
        sc.wait("vector", ld_wg.load[wslot], lwg)
        if first:
            sc.wait_ev("vector", ev_thr)
        sc.op("vector", lambda h, ws=wslot: h.tensor_scalar(
            c1_bf[:], wg_in[ws][:], thr_gu[:], None, OP.is_gt))
        ev_q = sc.op("vector", lambda h, ws=wslot, qs=qslot: h.scalar_tensor_tensor(
            wqn_bf[qs][:], wg_in[ws][:], nthr_gu[:], c1_bf[:], OP.is_lt, OP.subtract),
            tick=s_vec)
        ld_wg.release(ev_q)
        pend_q[(jp, which, c)] = (qslot, ev_q)

    def emit_build_tp(jp, which, c, buf):
        qslot, ev_q = pend_q.pop((jp, which, c))
        sc.wait_ev("tensor", ev_q)
        wqT_t = (wqT_g if which == 0 else wqT_u)[buf]
        for half in range(2):
            ps_ = r_tp.acquire("tensor")
            ev_tp = None
            for jj in range(8):
                kk = 8 * half + jj
                tk = s_pe if jj == 7 else None
                e = sc.op("tensor", lambda h, qs=qslot, ps=ps_, jj=jj, kk=kk: h.transpose(
                    pb[ps][:, 128 * jj:128 * (jj + 1)],
                    wqn_bf[qs][:, 128 * kk:128 * (kk + 1)], id_sb[:]), tick=tk)
                ev_tp = e or ev_tp
            if half == 1:
                r_wqn.release(ev_tp)
            sc.wait_ev("vector", ev_tp)
            dstw = sb_pkc(wqT_t, KD, 8 * half, 8 * half + 8, 128 * c, 128 * (c + 1))
            srcw = pb[ps_].rearrange("p (k c2) -> p k c2", k=8)
            ev_ev = sc.op("vector", lambda h, d=dstw, s2=srcw: h.tensor_copy(d, s2),
                          tick=s_vec)
            r_tp.release(ev_ev)

    def emit_build_chunk(jp, which, c, buf, first=False):
        emit_build_quant(jp, which, c, first=first)
        emit_build_tp(jp, which, c, buf)

    r_wqT.acquire("vector")
    for which in (0, 1):
        for c in (0, 1):
            emit_build_chunk(0, which, c, 0, first=(which == 0 and c == 0))
    build_done[0] = s_vec.n

    for jp in range(NP):
        buf = jp % 2
        sc.wait("tensor", s_vec, build_done[jp])
        r_gblk.acquire("vector")
        us_h2.acquire("vector")
        chunks = [(w, c) for w in (0, 1) for c in (0, 1)] if jp + 1 < NP else []
        qchunks = list(chunks)
        if chunks:
            r_wqT.acquire("vector")
        ev_h2 = None
        ev_gu_last = None
        for t in range(T):
            pp = r_gups.acquire("tensor")
            for k in range(KD):
                sc.op("tensor", lambda h, pp=pp, k=k, t=t, b=buf: h.matmul(
                    pf[2 * pp][:, 0:256], xqT_sb[:, S * k + 128 * t: S * k + 128 * (t + 1)],
                    wqT_g[b][:, 256 * k:256 * (k + 1)],
                    start=(k == 0), stop=(k == KD - 1)))
            ev_gu = None
            for k in range(KD):
                tk = s_pe if k == KD - 1 else None
                e = sc.op("tensor", lambda h, pp=pp, k=k, t=t, b=buf: h.matmul(
                    pf[2 * pp + 1][:, 0:256], xqT_sb[:, S * k + 128 * t: S * k + 128 * (t + 1)],
                    wqT_u[b][:, 256 * k:256 * (k + 1)],
                    start=(k == 0), stop=(k == KD - 1)), tick=tk)
                ev_gu = e or ev_gu
            ev_gu_last = ev_gu
            sc.wait_ev("vector", ev_gu)
            V3 = lambda fn, **kw: sc.op("vector", fn, **kw)
            V3(lambda h, pp=pp, t=t: h.tensor_scalar(
                g_blk[:], pf[2 * pp][:, 0:256], sg_all[:, t:t + 1], None, OP.mult))
            ev_gup = V3(lambda h, pp=pp, t=t: h.scalar_tensor_tensor(
                gup_blk[:], pf[2 * pp + 1][:, 0:256], sg_all[:, t:t + 1], g_blk[:],
                OP.mult, OP.mult), tick=s_vec)
            r_gups.release(ev_gup)
            ev_h2 = V3(lambda h, t=t: h.scalar_tensor_tensor(
                h2stg[:, 256 * t:256 * (t + 1)], g_blk[:], 0.0, gup_blk[:], OP.max, OP.mult),
                tick=s_vec)
            if qchunks and t in (1, 4, 7, 10):
                emit_build_quant(jp + 1, *qchunks.pop(0))
            if chunks and t in (3, 6, 9, 12):
                emit_build_tp(jp + 1, *chunks.pop(0), 1 - buf)
        build_done[jp + 1] = s_vec.n
        r_wqT.release(ev_gu_last)
        r_gblk.release(ev_h2)
        sc.wait("sync", s_vec, s_vec.n)
        dsth2 = hbm_pkc(h2_d, 0, T, 256 * jp, 256 * (jp + 1))
        srch2 = sb_pkc(h2stg, T, 0, T, 0, 256)
        us_h2.issue(dsth2, srch2)

    es_d.close()

    es_d2 = ExitStack()
    sbd2 = lambda nm, shape, dt: es_d2.enter_context(nc.sbuf_tensor(nm, shape, dt))
    h2row = [sbd2("h2row%d" % i, [128, FF], BF16) for i in range(2)]
    y2_f = sbd2("y2_f", [128, FF], F32)
    hq_bf = sbd2("hq_bf", [128, FF], BF16)
    hqst = [sbd2("hqst%d" % i, [128, KF * 128], BF16) for i in range(2)]

    fence("gpsimd", "sync", "vector", "scalar")
    us_h2.wait_all("vector")
    us_h2.wait_all("scalar")

    ld_h2 = DStream("ldh2", 2, "gpsimd")
    us_hq = UStore("us_hq", 2)
    r_y2 = Ring(1)
    r_hq = Ring(1)

    us_h2.wait_all("gpsimd")
    sfac2_all = st
    V4 = lambda fn, **kw: sc.op("vector", fn, **kw)
    SP4 = lambda **kw: sc.op("vector", lambda h: h.tensor_copy(
        spacer_t[:, 0:512], spacer_t[:, 512:1024]), **kw)
    evA = {}
    hslots = {}

    def emit_A(t):
        hnext = ld_h2.issued % 2
        hslot, lh2 = ld_h2.issue([(h2row[hnext][:], h2_d[128 * t:128 * (t + 1), :])])
        hslots[t] = hslot
        sc.wait("vector", ld_h2.load[hslot], lh2)
        V4(lambda h, s=hslot: h.tensor_reduce(
            st["mx2"][:], h2row[s][:], AX.X, OP.max, apply_absolute_value=True))
        SP4()
        V4(lambda h: h.tensor_scalar(st["cmx2"][:], st["mx2"][:], 1e-5, None, OP.max))
        SP4()
        V4(lambda h, t=t: h.tensor_scalar(rs2_all[:, t:t + 1], st["cmx2"][:], 1.0 / 127, None, OP.mult))
        SP4()
        V4(lambda h: h.reciprocal(st["icx2"][:], st["cmx2"][:]))
        SP4()
        V4(lambda h, t=t: h.tensor_scalar(sf2_all[:, t:t + 1], st["icx2"][:], 127.0, None, OP.mult))
        evA[t] = SP4(tick=s_vec)

    emit_A(0)
    for t in range(T):
        if t + 1 < T:
            emit_A(t + 1)
        hslot = hslots.pop(t)
        r_y2.acquire("scalar")
        sc.wait_ev("scalar", evA.pop(t))
        ev_y2 = sc.op("scalar", lambda h, s=hslot, t=t: h.activation(
            y2_f[:], h2row[s][:], AF.Identity, bias=c_rne_t[:], scale=sf2_all[:, t:t + 1]),
            tick=s_act)
        ld_h2.release(ev_y2)
        r_hq.acquire("vector")
        sc.wait_ev("vector", ev_y2)
        ev_hq = V4(lambda h: h.tensor_scalar(hq_bf[:], y2_f[:], C_RNE, None, OP.subtract),
                   tick=s_vec)
        r_y2.release(ev_hq)
        qslot = us_hq.acquire("scalar")
        sc.wait_ev("tensor", ev_hq)
        done = 0
        while done < KF:
            cnt = min(8, KF - done)
            ps_ = r_tp.acquire("tensor")
            ev_tp = None
            for jj in range(cnt):
                kk = done + jj
                tk = s_pe if jj == cnt - 1 else None
                e = sc.op("tensor", lambda h, ps=ps_, jj=jj, kk=kk: h.transpose(
                    pb[ps][:, 128 * jj:128 * (jj + 1)], hq_bf[:, 128 * kk:128 * (kk + 1)],
                    id_sb[:]), tick=tk)
                ev_tp = e or ev_tp
            if done + cnt >= KF:
                r_hq.release(ev_tp)
            sc.wait_ev("scalar", ev_tp)
            dsts = hqst[qslot][:, 128 * done:128 * (done + cnt)]
            ev_ev = sc.op("scalar", lambda h, ps=ps_, d=dsts, c=cnt: h.activation(
                d, pb[ps][:, 0:128 * c], AF.Identity), tick=s_act)
            r_tp.release(ev_ev)
            done += cnt
        sc.wait("sync", s_act, s_act.n)
        dsthq = hqT_d[t].rearrange("(k p) c -> p k c", p=128)
        srchq = sb_pkc(hqst[qslot], KF, 0, KF, 0, 128)
        us_hq.issue(dsthq, srchq)

    es_d2.close()

    es_e = ExitStack()
    sbe = lambda nm, shape, dt: es_e.enter_context(nc.sbuf_tensor(nm, shape, dt))
    wdT_q = sbe("wdT_q", [128, KF * 512], BF16)
    wd_in = [sbe("wd_in%d" % i, [128, 2816], F32) for i in range(2)]
    c1d_bf = sbe("c1d_bf", [128, 2816], BF16)
    wqnd_bf = [sbe("wqnd%d" % i, [128, 2816], BF16) for i in range(2)]
    hq_in = [sbe("hq_in%d" % i, [128, KF * 128], BF16) for i in range(2)]
    x5_in = [sbe("x5_in%d" % i, [128, 512], F32) for i in range(2)]
    outst = [sbe("outst%d" % i, [128, 512], F32) for i in range(2)]
    ffnst = [sbe("ffnst%d" % i, [128, 512], F32) for i in range(2)]

    fence("gpsimd", "sync", "vector", "scalar")
    us_hq.wait_all("gpsimd")
    us_hq.wait_all("vector")
    us_hq.wait_all("scalar")

    ld_wd = DStream("ldwd", 2, "gpsimd")
    ld_hq = DStream("ldhq", 2, "sync")
    ld_x5 = DStream("ldx5", 2, "gpsimd")
    us_out = UStore("us_out", 2, qeng="scalar")
    r_wqnd = Ring(2)
    r_dnps = Ring(2)
    r_wdT = Ring(1)

    us_hq.wait_all("sync")
    sc.op("vector", lambda h: h.tensor_copy(spacer_t[:, 0:512], spacer_t[:, 512:1024]))
    sc.op("vector", lambda h: h.tensor_scalar(
        sd_all[:], rs2_all[:], ncm_dn[:], None, OP.mult))
    for q in range(4):
        r_wdT.acquire("vector")
        for c in range(4):
            for hh in range(2):
                wnext = ld_wd.issued % 2
                wslot, lwd = ld_wd.issue([(wd_in[wnext][:],
                                           w_dn[512 * q + 128 * c:512 * q + 128 * (c + 1),
                                                2816 * hh:2816 * (hh + 1)])])
                qslot = r_wqnd.acquire("vector")
                sc.wait("vector", ld_wd.load[wslot], lwd)
                sc.op("vector", lambda h, ws=wslot: h.tensor_scalar(
                    c1d_bf[:], wd_in[ws][:], thr_dn[:], None, OP.is_gt))
                ev_qd = sc.op("vector", lambda h, ws=wslot, qs=qslot: h.scalar_tensor_tensor(
                    wqnd_bf[qs][:], wd_in[ws][:], nthr_dn[:], c1d_bf[:], OP.is_lt, OP.subtract),
                    tick=s_vec)
                ld_wd.release(ev_qd)
                sc.wait_ev("tensor", ev_qd)
                done = 0
                while done < 22:
                    cnt = min(8, 22 - done)
                    ps_ = r_tp.acquire("tensor")
                    ev_tp = None
                    for jj in range(cnt):
                        kk = done + jj
                        tk = s_pe if jj == cnt - 1 else None
                        e = sc.op("tensor", lambda h, qs=qslot, ps=ps_, jj=jj, kk=kk: h.transpose(
                            pb[ps][:, 128 * jj:128 * (jj + 1)],
                            wqnd_bf[qs][:, 128 * kk:128 * (kk + 1)], id_sb[:]), tick=tk)
                        ev_tp = e or ev_tp
                    if done + cnt >= 22:
                        r_wqnd.release(ev_tp)
                    sc.wait_ev("scalar", ev_tp)
                    dstw = sb_pkc(wdT_q, KF, 22 * hh + done, 22 * hh + done + cnt,
                                  128 * c, 128 * (c + 1))
                    srcw = pb[ps_].rearrange("p (k c2) -> p k c2", k=8)[:, 0:cnt]
                    ev_ev = sc.op("scalar", lambda h, d=dstw, s=srcw: h.activation(
                        d, s, AF.Identity), tick=s_act)
                    r_tp.release(ev_ev)
                    done += cnt
        v_wdT = s_act.n
        sc.wait("tensor", s_act, v_wdT)
        ev_dn = None
        for t in range(T):
            hnext = ld_hq.issued % 2
            hslot, lhq = ld_hq.issue([(hq_in[hnext][:],
                                       hqT_d[t].rearrange("(k p) c -> p k c", p=128))])
            xnext = ld_x5.issued % 2
            xslot, lx5 = ld_x5.issue([(x5_in[xnext][:],
                                       x_in[128 * t:128 * (t + 1), 512 * q:512 * (q + 1)])])
            pslot = r_dnps.acquire("tensor")
            sc.wait("tensor", ld_hq.load[hslot], lhq)
            ev_dn = None
            for hk in range(KF):
                tk = s_pe if hk == KF - 1 else None
                e = sc.op("tensor", lambda h, ps=pslot, hk=hk, hs=hslot: h.matmul(
                    pf[ps][:], hq_in[hs][:, 128 * hk:128 * (hk + 1)],
                    wdT_q[:, 512 * hk:512 * (hk + 1)],
                    start=(hk == 0), stop=(hk == KF - 1)), tick=tk)
                ev_dn = e or ev_dn
            ld_hq.release(ev_dn)
            oslot = us_out.acquire("vector")
            sc.wait_ev("scalar", ev_dn)
            ev_f = sc.op("scalar", lambda h, ps=pslot, os=oslot, t=t: h.activation(
                ffnst[os][:], pf[ps][:], AF.Identity, scale=sd_all[:, t:t + 1]),
                tick=s_act)
            r_dnps.release(ev_f)
            sc.wait("vector", ld_x5.load[xslot], lx5)
            V5 = lambda fn, **kw: sc.op("vector", fn, **kw)
            ev_e1 = V5(lambda h, xs=xslot, os=oslot, t=t, q=q: h.scalar_tensor_tensor(
                outst[os][:], x5_in[xs][:], rinv1_all[:, t:t + 1],
                acc[:, D * t + 512 * q: D * t + 512 * (q + 1)], OP.mult, OP.add), tick=s_vec)
            ld_x5.release(ev_e1)
            ev_e2 = V5(lambda h, os=oslot, t=t: h.tensor_scalar(
                outst[os][:], outst[os][:], negmr_all[:, t:t + 1], None, OP.add), tick=s_vec)
            sc.wait_ev("vector", ev_f)
            ev_e3 = V5(lambda h, os=oslot: h.tensor_tensor(
                outst[os][:], outst[os][:], ffnst[os][:], OP.add), tick=s_vec)
            sc.wait_ev("scalar", ev_e3)
            us_out.issue(out[128 * t:128 * (t + 1), 512 * q:512 * (q + 1)], outst[oslot][:])
            if debug:
                sc.wait_ev("sync", ev_f)
                sc.op("sync", lambda h, os=oslot, t=t, q=q: h.dma_start(
                    dbg_ffn[128 * t:128 * (t + 1), 512 * q:512 * (q + 1)], ffnst[os][:]),
                    tick=us_out.st[oslot], inc=16)
        r_wdT.release(ev_dn)

    if debug:
        s_dbg = ESem("s_dbg")
        sc.wait("sync", s_vec, s_vec.n)
        sc.wait("sync", s_act, s_act.n)
        for dd, ss in [(dbg_p1, dbg1_sb[:]), (dbg_acc, acc[:]), (dbg_qT, qT[:]), (dbg_kT, kT[:]),
                       (dbg_st[:, 0:T], rinv1_all[:]), (dbg_st[:, T:2 * T], negmr_all[:]),
                       (dbg_st[:, 2 * T:3 * T], rs1_all[:]), (dbg_st[:, 3 * T:4 * T], rs2_all[:]),
                       (dbg_st[:, 4 * T:4 * T + 2], cmb[:]),
                       (dbg_st[:, 4 * T + 2:4 * T + 3], thr_gu[:]),
                       (dbg_st[:, 4 * T + 3:4 * T + 4], thr_dn[:])]:
            sc.op("sync", lambda h, d=dd, s2=ss: h.dma_start(d, s2, allow_slow_non_contiguous=True), tick=s_dbg, inc=16)
        sc.wait("sync", s_dbg, s_dbg.n)

    us_out.wait_all("scalar")

    es_e.close()
    es_r2.close()

    block = es_p.enter_context(nc.Block())
    for eng in ENGINES:
        thunks = sc.q[eng]
        if not thunks:
            continue
        getattr(block, eng)(lambda h, th=thunks: [f(h) for f in th])
    es_p.close()
    return nc


_NC_CACHE = {}


def _get_nc(debug=False):
    key = bool(debug)
    if key not in _NC_CACHE:
        _NC_CACHE[key] = build_program(debug=key)
    return _NC_CACHE[key]


def kernel(L0_attn_x, L0_attn_W_hidden, L0_attn_b_hidden, L0_attn_W_qk,
           L0_attn_b_qk, L0_attn_gamma_q, L0_attn_beta_q, L0_attn_gamma_k,
           L0_attn_beta_k, L0_attn_W_out, L0_attn_b_out, L0_ffn_w_gate_up,
           L0_ffn_w_down, L0_ffn_norm_weight):
    x = np.ascontiguousarray(np.asarray(L0_attn_x, dtype=np.float32))
    com = {
        "w_hid": np.ascontiguousarray(np.asarray(L0_attn_W_hidden, dtype=np.float32)),
        "b_hid": np.asarray(L0_attn_b_hidden, dtype=np.float32).reshape(1, HID2),
        "w_qk": np.ascontiguousarray(np.asarray(L0_attn_W_qk, dtype=np.float32)),
        "b_qk": np.asarray(L0_attn_b_qk, dtype=np.float32).reshape(QKD, 1),
        "g_q": np.asarray(L0_attn_gamma_q, dtype=np.float32).reshape(QKD, 1),
        "be_q": np.asarray(L0_attn_beta_q, dtype=np.float32).reshape(QKD, 1),
        "g_k": np.asarray(L0_attn_gamma_k, dtype=np.float32).reshape(QKD, 1),
        "be_k": np.asarray(L0_attn_beta_k, dtype=np.float32).reshape(QKD, 1),
        "w_out": np.ascontiguousarray(np.asarray(L0_attn_W_out, dtype=np.float32)),
        "b_out": np.asarray(L0_attn_b_out, dtype=np.float32).reshape(1, D),
        "w_gu": np.ascontiguousarray(np.asarray(L0_ffn_w_gate_up, dtype=np.float32)),
        "w_dn": np.ascontiguousarray(np.asarray(L0_ffn_w_down, dtype=np.float32)),
        "nw": np.asarray(L0_ffn_norm_weight, dtype=np.float32).reshape(1, D),
        "ident": np.eye(128, dtype=ml_dtypes.bfloat16),
        "ones_bf": np.ones((1, 128), dtype=ml_dtypes.bfloat16),
        "ones_f": np.ones((128, 1), dtype=np.float32),
        "ones_rf": np.ones((1, 128), dtype=np.float32),
    }
    in_maps = [dict(com, x=np.ascontiguousarray(x[b])) for b in range(8)]
    nc = _get_nc(debug=False)
    res = run_bass_kernel_spmd(nc, in_maps, core_ids=list(range(8)))
    return np.stack([np.asarray(r["out"], dtype=np.float32) for r in res.results], axis=0)


# version 19
# speedup vs baseline: 1.0338x; 1.0338x over previous
import numpy as np
import ml_dtypes
from contextlib import ExitStack

import concourse.bass as bass
import concourse.mybir as mybir
from concourse.bass_utils import run_bass_kernel_spmd
from concourse.alu_op_type import AluOpType

F32 = mybir.dt.float32
BF16 = mybir.dt.bfloat16
AF = mybir.ActivationFunctionType
AX = mybir.AxisListType
OP = AluOpType

S = 2048
D = 2048
HID2 = 8192
HV = 4096
QKD = 128
FF = 5632
OGU = 2 * FF
T = S // 128
KD = D // 128
KF = FF // 128
EPS = 1e-5
C_RNE = float(np.float32(1.5 * 2 ** 23))
INV_S2 = 1.0 / float(S) ** 2

ENGINES = ("tensor", "vector", "scalar", "sync", "gpsimd")


def build_program(debug=False, paranoid=False):
    nc = bass.Bass("TRN2", target_bir_lowering=False, detect_race_conditions=False)

    x_in = nc.dram_tensor("x", [S, D], F32, kind="ExternalInput").ap()
    w_hid = nc.dram_tensor("w_hid", [D, HID2], F32, kind="ExternalInput").ap()
    b_hid = nc.dram_tensor("b_hid", [1, HID2], F32, kind="ExternalInput").ap()
    w_qk = nc.dram_tensor("w_qk", [D, QKD], F32, kind="ExternalInput").ap()
    b_qk = nc.dram_tensor("b_qk", [QKD, 1], F32, kind="ExternalInput").ap()
    g_q = nc.dram_tensor("g_q", [QKD, 1], F32, kind="ExternalInput").ap()
    be_q = nc.dram_tensor("be_q", [QKD, 1], F32, kind="ExternalInput").ap()
    g_k = nc.dram_tensor("g_k", [QKD, 1], F32, kind="ExternalInput").ap()
    be_k = nc.dram_tensor("be_k", [QKD, 1], F32, kind="ExternalInput").ap()
    w_out = nc.dram_tensor("w_out", [HV, D], F32, kind="ExternalInput").ap()
    b_out = nc.dram_tensor("b_out", [1, D], F32, kind="ExternalInput").ap()
    w_gu = nc.dram_tensor("w_gu", [OGU, D], F32, kind="ExternalInput").ap()
    w_dn = nc.dram_tensor("w_dn", [D, FF], F32, kind="ExternalInput").ap()
    nw = nc.dram_tensor("nw", [1, D], F32, kind="ExternalInput").ap()
    ident = nc.dram_tensor("ident", [128, 128], BF16, kind="ExternalInput").ap()
    ones_bf = nc.dram_tensor("ones_bf", [1, 128], BF16, kind="ExternalInput").ap()
    ones_f = nc.dram_tensor("ones_f", [128, 1], F32, kind="ExternalInput").ap()
    ones_rf = nc.dram_tensor("ones_rf", [1, 128], F32, kind="ExternalInput").ap()

    out = nc.dram_tensor("out", [S, D], F32, kind="ExternalOutput").ap()

    def scratch(name, shape, dt):
        if debug:
            return nc.dram_tensor(name, shape, dt, kind="ExternalOutput").ap()
        return nc.dram_tensor(name, shape, dt).ap()

    hid_d = scratch("hid_d", [S, HID2], BF16)
    xqT_d = scratch("xqT_d", [D, S], BF16)
    h2_d = scratch("h2_d", [S, FF], BF16)
    hqT_d = scratch("hqT_d", [T, FF, 128], BF16)
    if debug:
        dbg_ffn = nc.dram_tensor("dbg_ffn", [S, D], F32, kind="ExternalOutput").ap()
        dbg_p1 = nc.dram_tensor("dbg_p1", [128, 8 * T], F32, kind="ExternalOutput").ap()
        dbg_acc = nc.dram_tensor("dbg_acc", [128, T * D], BF16, kind="ExternalOutput").ap()
        dbg_qT = nc.dram_tensor("dbg_qT", [128, S], BF16, kind="ExternalOutput").ap()
        dbg_kT = nc.dram_tensor("dbg_kT", [128, S], BF16, kind="ExternalOutput").ap()
        dbg_st = nc.dram_tensor("dbg_st", [128, 4 * T + 8], F32, kind="ExternalOutput").ap()

    es_p = ExitStack()

    class ESem:
        def __init__(self, name):
            self.h = es_p.enter_context(nc.semaphore(name))
            self.n = 0

    class Sched:
        def __init__(self):
            self.q = {e: [] for e in ENGINES}
            self.psems = None

        def op(self, eng, fn, tick=None, inc=1):
            if self.psems is not None and eng in ("tensor", "vector", "scalar"):
                for oe, sem_ in self.psems.items():
                    if oe != eng and sem_.n > 0:
                        self.wait(eng, sem_, sem_.n)
                if tick is None:
                    tick = self.psems[eng]
                    inc = 1
            cap = None
            if tick is not None:
                tick.n += inc
                cap = (tick.h, inc)

            def thunk(h, fn=fn, cap=cap):
                ins = fn(h)
                if cap is not None:
                    ins.then_inc(cap[0], cap[1])
            self.q[eng].append(thunk)
            return (tick, tick.n) if tick is not None else None

        def wait(self, eng, sem, tgt):
            if tgt <= 0:
                return
            self.q[eng].append(lambda h, s=sem.h, t=tgt: h.wait_ge(s, t))

        def wait_ev(self, eng, ev):
            if ev is not None:
                self.wait(eng, ev[0], ev[1])

    sc = Sched()

    class Ring:
        def __init__(self, nslots):
            self.nslots = nslots
            self.acq = 0
            self.rels = []

        def acquire(self, eng):
            k = self.acq
            self.acq += 1
            if k >= self.nslots:
                idx = k - self.nslots
                assert idx < len(self.rels), "ring release missing (acq %d)" % k
                for ev in self.rels[idx]:
                    sc.wait_ev(eng, ev)
            return k % self.nslots

        def release(self, *evs):
            self.rels.append([e for e in evs if e is not None])

    class DStream:
        def __init__(self, name, nslots, qeng):
            self.load = [ESem(name + "_l%d" % i) for i in range(nslots)]
            self.nslots = nslots
            self.qeng = qeng
            self.issued = 0
            self.rels = []

        def issue(self, pairs):
            k = self.issued
            slot = k % self.nslots
            if k >= self.nslots:
                idx = k - self.nslots
                assert idx < len(self.rels), "dstream release missing (%d)" % k
                for ev in self.rels[idx]:
                    sc.wait_ev(self.qeng, ev)
            for o, i in pairs:
                sc.op(self.qeng, lambda h, o=o, i=i: h.dma_start(o, i),
                      tick=self.load[slot], inc=16)
            self.issued += 1
            return slot, self.load[slot].n

        def release(self, *evs):
            self.rels.append([e for e in evs if e is not None])

    class UStore:
        def __init__(self, name, nslots, qeng="sync"):
            self.st = [ESem(name + "_s%d" % i) for i in range(nslots)]
            self.nslots = nslots
            self.qeng = qeng
            self.n = 0

        def acquire(self, eng):
            k = self.n
            slot = k % self.nslots
            if k >= self.nslots:
                sc.wait(eng, self.st[slot], self.st[slot].n)
            return slot

        def issue(self, dst, src):
            slot = self.n % self.nslots
            sc.op(self.qeng, lambda h, d=dst, s=src: h.dma_start(d, s),
                  tick=self.st[slot], inc=16)
            self.n += 1

        def wait_all(self, eng):
            for s_ in self.st:
                sc.wait(eng, s_, s_.n)

    s_pe = ESem("s_pe")
    s_vec = ESem("s_vec")
    s_act = ESem("s_act")
    s_win = [ESem("s_win%d" % i) for i in range(4)]
    if paranoid:
        sc.psems = {"tensor": s_pe, "vector": s_vec, "scalar": s_act}

    def fence(*engs):
        tg = {"tensor": s_pe.n, "vector": s_vec.n, "scalar": s_act.n}
        for e in engs:
            for oe, sem_ in (("tensor", s_pe), ("vector", s_vec), ("scalar", s_act)):
                if oe != e and tg[oe] > 0:
                    sc.wait(e, sem_, tg[oe])

    sbp = lambda nm, shape, dt: es_p.enter_context(nc.sbuf_tensor(nm, shape, dt))
    qT = sbp("qT", [128, S], BF16)
    kT = sbp("kT", [128, S], BF16)
    id_sb = sbp("id_sb", [128, 128], BF16)
    onesb_sb = sbp("onesb_sb", [1, 128], BF16)
    onesf_sb = sbp("onesf_sb", [128, 1], F32)
    onesrf_sb = sbp("onesrf_sb", [1, 128], F32)
    bqk_sb = sbp("bqk_sb", [QKD, 1], F32)
    gq_sb = sbp("gq_sb", [QKD, 1], F32)
    bq_sb = sbp("bq_sb", [QKD, 1], F32)
    gk_sb = sbp("gk_sb", [QKD, 1], F32)
    bk_sb = sbp("bk_sb", [QKD, 1], F32)
    rinv1_all = sbp("rinv1_all", [128, T], F32)
    negmr_all = sbp("negmr_all", [128, T], F32)
    rs1_all = sbp("rs1_all", [128, T], F32)
    rs2_all = sbp("rs2_all", [128, T], F32)
    c_rne_t = sbp("c_rne_t", [128, 1], F32)
    invs2_t = sbp("invs2_t", [128, 1], F32)
    spacer_t = sbp("spacer_t", [128, 1024], F32)
    abscols = sbp("abscols", [128, 288], F32)
    thr_gu = sbp("thr_gu", [128, 1], F32)
    nthr_gu = sbp("nthr_gu", [128, 1], F32)
    ncm_gu = sbp("ncm_gu", [128, 1], F32)
    thr_dn = sbp("thr_dn", [128, 1], F32)
    nthr_dn = sbp("nthr_dn", [128, 1], F32)
    ncm_dn = sbp("ncm_dn", [128, 1], F32)
    red_tmp = sbp("red_tmp", [128, 2], F32)
    cm_row = sbp("cm_row", [1, 2], F32)
    cmb = sbp("cmb", [128, 2], F32)
    sg_col = sbp("sg_col", [128, 1], F32)
    sd_col = sbp("sd_col", [128, 1], F32)
    sg_all = sbp("sg_all", [128, T], F32)
    sd_all = sbp("sd_all", [128, T], F32)
    trashw = sbp("trashw", [128, 1024], BF16)
    sf2_all = sbp("sf2_all", [128, T], F32)
    zero_bf = sbp("zero_bf", [128, 128], BF16)
    bnst_t = sbp("bnst_t", [128, 24], F32)
    dbg1_sb = sbp("dbg1_sb", [128, 8 * T], F32) if debug else None
    bnag_t = sbp("bnag_t", [128, 2], F32)
    stn = ("sum", "sq", "m", "msq", "m2", "varp", "vpe", "rinv2", "rinv",
           "negmr", "msv", "mse", "r2a", "rmsinv", "mx", "cmx", "icx", "sfac",
           "mx2", "cmx2", "icx2", "sfac2")
    st = {nm: sbp("st_" + nm, [128, 1], F32) for nm in stn}

    pf = [es_p.enter_context(nc.psum_tensor("pf%d" % i, [128, 512], F32)) for i in range(6)]
    pb = [es_p.enter_context(nc.psum_tensor("pb%d" % i, [128, 1024], BF16)) for i in range(2)]

    r_tp = Ring(2)

    def hbm_pkc(ap2d, a0, a1, c0, c1):
        return ap2d.rearrange("(a p) c -> p a c", p=128)[:, a0:a1, c0:c1]

    def sb_pkc(t, k, k0, k1, c0, c1):
        return t.rearrange("p (k c) -> p k c", k=k)[:, k0:k1, c0:c1]

    ld0 = DStream("c0", 1, "gpsimd")
    _, n0 = ld0.issue([
        (id_sb[:], ident[:, :]), (onesb_sb[:], ones_bf[:, :]),
        (onesf_sb[:], ones_f[:, :]), (onesrf_sb[:], ones_rf[:, :]),
        (bqk_sb[:], b_qk[:, :]), (gq_sb[:], g_q[:, :]), (bq_sb[:], be_q[:, :]),
        (gk_sb[:], g_k[:, :]), (bk_sb[:], be_k[:, :])])
    s_mem0 = ESem("s_mem0")
    if debug:
        sc.op("gpsimd", lambda h: h.memset(dbg1_sb[:], 0.0))
    sc.op("gpsimd", lambda h: h.memset(zero_bf[:], 0.0))
    sc.op("gpsimd", lambda h: h.memset(c_rne_t[:], C_RNE))
    sc.op("gpsimd", lambda h: h.memset(invs2_t[:], INV_S2))
    sc.op("gpsimd", lambda h: h.memset(spacer_t[:], 1.0), tick=s_mem0)

    es_r1 = ExitStack()
    xnT = es_r1.enter_context(nc.sbuf_tensor("xnT", [128, KD * S], BF16))

    es_a = ExitStack()
    sba = lambda nm, shape, dt: es_a.enter_context(nc.sbuf_tensor(nm, shape, dt))
    g_bcast = sba("g_bcast", [128, D], F32)
    nw_row = sba("nw_row", [1, D], F32)
    x_sb = [sba("x_sb%d" % i, [128, D], F32) for i in range(3)]
    bnag2 = [sba("bnag2_%d" % i, [128, 2], F32) for i in range(2)]
    rinv2_2 = [sba("rinv2_2_%d" % i, [128, 1], F32) for i in range(2)]
    rinv_2 = [sba("rinv_2_%d" % i, [128, 1], F32) for i in range(2)]
    rmsinv_2 = [sba("rmsinv_2_%d" % i, [128, 1], F32) for i in range(2)]
    sfac_2 = [sba("sfac_2_%d" % i, [128, 1], F32) for i in range(2)]
    xn_sb = [sba("xn_sb%d" % i, [128, D], BF16) for i in range(2)]
    xnorm_sb = [sba("xnorm%d" % i, [128, D], F32) for i in range(2)]
    y_sb = [sba("y_sb%d" % i, [128, D], F32) for i in range(2)]
    xq_sb = [sba("xq_sb%d" % i, [128, D], BF16) for i in range(2)]
    xqst_sb = [sba("xqst%d" % i, [128, D], BF16) for i in range(2)]
    trash1 = sba("trash1", [128, D], BF16)
    trash2 = sba("trash2", [128, D], BF16)

    ld_x = DStream("ldx", 3, "gpsimd")
    ld_nw = DStream("ldnw", 1, "gpsimd")
    r_xnorm = Ring(2)
    r_y = Ring(2)
    r_xn = Ring(2)
    us_xq = UStore("us_xq", 2)

    _, nnw = ld_nw.issue([(nw_row[:], nw[:, :])])
    sc.wait("tensor", ld_nw.load[0], nnw)
    sc.wait("tensor", ld0.load[0], n0)
    r_gb = Ring(2)
    for j in range(D // 512):
        slot = r_gb.acquire("tensor")
        ev = sc.op("tensor", lambda h, s=slot, j=j: h.matmul(
            pf[s][:], onesrf_sb[:], nw_row[:, 512 * j:512 * (j + 1)],
            start=True, stop=True), tick=s_pe)
        sc.wait_ev("vector", ev)
        ev_c = sc.op("vector", lambda h, s=slot, j=j: h.tensor_copy(
            g_bcast[:, 512 * j:512 * (j + 1)], pf[s][:]), tick=s_vec)
        r_gb.release(ev_c)
    v_gb = s_vec.n

    V = lambda fn, **kw: sc.op("vector", fn, **kw)
    SP = lambda **kw: sc.op("vector", lambda h: h.tensor_copy(
        spacer_t[:, 0:512], spacer_t[:, 512:1024]), **kw)
    p1 = {}

    def p1_S1a(t):
        s = t % 2
        xslot, lx = ld_x.issue([(x_sb[t % 3][:], x_in[128 * t:128 * (t + 1), :])])
        p1[("xs", t)] = xslot
        sc.wait("vector", ld_x.load[xslot], lx)
        if t == 0:
            sc.wait("vector", s_vec, v_gb)
        for j in range(4):
            V(lambda h, xs=xslot, j=j: h.bn_stats(
                bnst_t[:, 6 * j:6 * (j + 1)], x_sb[xs][:, 512 * j:512 * (j + 1)]))
        SP()
        V(lambda h, s=s: h.bn_aggr(bnag2[s][:], bnst_t[:]))
        SP()
        V(lambda h, s=s: h.tensor_scalar(st["vpe"][:], bnag2[s][:, 1:2], EPS, None, OP.add))
        SP()
        V(lambda h, s=s: h.reciprocal(rinv2_2[s][:], st["vpe"][:]))
        ev_r2 = SP(tick=s_vec)
        sc.wait_ev("scalar", ev_r2)
        p1[("ri", t)] = sc.op("scalar", lambda h, s=s: h.activation(
            rinv_2[s][:], rinv2_2[s][:], AF.Sqrt), tick=s_act)

    def p1_S1b(t):
        s = t % 2
        xslot = p1[("xs", t)]
        sc.wait_ev("vector", p1[("ri", t)])
        if debug:
            for di, src_ap in enumerate((bnag2[s][:, 1:2], rinv2_2[s][:], rinv_2[s][:])):
                V(lambda h, t=t, di=di, a=src_ap: h.tensor_copy(
                    dbg1_sb[:, 8 * t + di:8 * t + di + 1], a))
        V(lambda h, s=s, t=t: h.tensor_scalar(rinv1_all[:, t:t + 1], rinv_2[s][:], 1.0, None, OP.add))
        ev_nm = V(lambda h, s=s, t=t: h.tensor_scalar(
            negmr_all[:, t:t + 1], bnag2[s][:, 0:1], rinv_2[s][:], -1.0, OP.mult, OP.mult),
            tick=s_vec)
        V(lambda h, s=s: h.scalar_tensor_tensor(st["msv"][:], bnag2[s][:, 1:2], 1.0, rinv2_2[s][:], OP.mult, OP.mult))
        SP()
        V(lambda h: h.tensor_scalar(st["mse"][:], st["msv"][:], EPS, None, OP.add))
        SP()
        V(lambda h: h.reciprocal(st["r2a"][:], st["mse"][:]))
        ev_ra = SP(tick=s_vec)
        sc.wait_ev("scalar", ev_ra)
        p1[("rms", t)] = sc.op("scalar", lambda h, s=s: h.activation(
            rmsinv_2[s][:], st["r2a"][:], AF.Sqrt), tick=s_act)
        xnslot = r_xn.acquire("scalar")
        p1[("xn_s", t)] = xnslot
        sc.wait_ev("scalar", ev_nm)
        ev_xn = sc.op("scalar", lambda h, xs=xslot, ns=xnslot, s=s, t=t: h.activation(
            xn_sb[ns][:], x_sb[xs][:], AF.Identity, bias=negmr_all[:, t:t + 1], scale=rinv_2[s][:]),
            tick=s_act)
        p1[("xn", t)] = ev_xn
        ld_x.release(ev_xn)

    def p1_S2a(t):
        s = t % 2
        xnslot = p1[("xn_s", t)]
        sc.wait_ev("tensor", p1[("xn", t)])
        evs_tp = []
        for half in range(2):
            ps_ = r_tp.acquire("tensor")
            ev_tp = None
            for j in range(8):
                jj = 8 * half + j
                tk = s_pe if j == 7 else None
                e = sc.op("tensor", lambda h, ns=xnslot, ps=ps_, j=j, jj=jj: h.transpose(
                    pb[ps][:, 128 * j:128 * (j + 1)], xn_sb[ns][:, 128 * jj:128 * (jj + 1)],
                    id_sb[:]), tick=tk)
                ev_tp = e or ev_tp
            sc.wait_ev("scalar", ev_tp)
            dst = sb_pkc(xnT, KD, 8 * half, 8 * half + 8, 128 * t, 128 * (t + 1))
            src = pb[ps_].rearrange("p (k c) -> p k c", k=8)
            ev_ev = sc.op("scalar", lambda h, d=dst, s2=src: h.activation(
                d, s2, AF.Identity), tick=s_act)
            r_tp.release(ev_ev)
            evs_tp.append(ev_tp)

        nslot = r_xnorm.acquire("vector")
        sc.wait_ev("vector", p1[("rms", t)])
        V(lambda h, ns=xnslot, s2=nslot, s=s: h.scalar_tensor_tensor(
            xnorm_sb[s2][:], xn_sb[ns][:], rmsinv_2[s][:], g_bcast[:], OP.mult, OP.mult))
        V(lambda h, s2=nslot: h.tensor_reduce(
            st["mx"][:], xnorm_sb[s2][:], AX.X, OP.max, apply_absolute_value=True))
        SP()
        V(lambda h: h.tensor_scalar(st["cmx"][:], st["mx"][:], 1e-5, None, OP.max))
        SP()
        V(lambda h, t=t: h.tensor_scalar(rs1_all[:, t:t + 1], st["cmx"][:], 1.0 / 127, None, OP.mult))
        SP()
        V(lambda h: h.reciprocal(st["icx"][:], st["cmx"][:]))
        SP()
        V(lambda h, s=s: h.tensor_scalar(sfac_2[s][:], st["icx"][:], 127.0, None, OP.mult))
        ev_sf = SP(tick=s_vec)
        r_xn.release(ev_sf, evs_tp[1])
        yslot = r_y.acquire("scalar")
        p1[("y_s", t)] = yslot
        sc.wait_ev("scalar", ev_sf)
        if t == 0:
            sc.wait("scalar", s_mem0, 1)
        p1[("y", t)] = sc.op("scalar", lambda h, s2=nslot, ys=yslot, s=s: h.activation(
            y_sb[ys][:], xnorm_sb[s2][:], AF.Identity, bias=c_rne_t[:], scale=sfac_2[s][:]),
            tick=s_act)
        r_xnorm.release(p1[("y", t)])

    def p1_S2b(t):
        sq = t % 2
        yslot = p1[("y_s", t)]
        sc.wait_ev("vector", p1[("y", t)])
        ev_xq = V(lambda h, ys=yslot, sq=sq: h.tensor_scalar(
            xq_sb[sq][:], y_sb[ys][:], C_RNE, None, OP.subtract), tick=s_vec)
        r_y.release(ev_xq)

        stslot = us_xq.acquire("scalar")
        sc.wait_ev("tensor", ev_xq)
        for half in range(2):
            ps_ = r_tp.acquire("tensor")
            ev_tp = None
            for j in range(8):
                jj = 8 * half + j
                tk = s_pe if j == 7 else None
                e = sc.op("tensor", lambda h, sq=sq, ps=ps_, j=j, jj=jj: h.transpose(
                    pb[ps][:, 128 * j:128 * (j + 1)], xq_sb[sq][:, 128 * jj:128 * (jj + 1)],
                    id_sb[:]), tick=tk)
                ev_tp = e or ev_tp
            sc.wait_ev("scalar", ev_tp)
            ev_ev = sc.op("scalar", lambda h, ps=ps_, ss=stslot, hf=half: h.activation(
                xqst_sb[ss][:, 1024 * hf:1024 * (hf + 1)], pb[ps][:], AF.Identity), tick=s_act)
            r_tp.release(ev_ev)
        sc.wait("sync", s_act, s_act.n)
        dstq = hbm_pkc(xqT_d, 0, KD, 128 * t, 128 * (t + 1))
        srcq = sb_pkc(xqst_sb[stslot], KD, 0, KD, 0, 128)
        us_xq.issue(dstq, srcq)

    p1_S1a(0)
    p1_S1a(1)
    for t in range(T):
        p1_S1b(t)
        if t + 2 < T:
            p1_S1a(t + 2)
        if t > 0:
            p1_S2b(t - 1)
        p1_S2a(t)
    p1_S2b(T - 1)
    v_p1 = s_vec.n

    es_a.close()

    es_b = ExitStack()
    sbb = lambda nm, shape, dt: es_b.enter_context(nc.sbuf_tensor(nm, shape, dt))
    whb = [sbb("whb%d" % i, [128, KD * 512], BF16) for i in range(2)]
    hstg = sbb("hstg0", [128, T * 512], BF16)
    qk_bf = sbb("qk_bf", [128, 512], BF16)
    wqk_b = sbb("wqk_b", [128, KD * 128], BF16)
    bhid_row = sbb("bhid_row", [1, HID2], BF16)

    fence("gpsimd", "sync", "scalar")
    us_xq.wait_all("gpsimd")
    us_xq.wait_all("scalar")

    ld_bh = DStream("ldbh", 1, "gpsimd")
    ld_qk = DStream("ldqk", 1, "gpsimd")
    ld_wh = DStream("ldwh", 2, "gpsimd")
    us_hid = UStore("us_hid", 1)
    r_hps = Ring(4)
    r_qkps = Ring(2)

    _, nbh = ld_bh.issue([(bhid_row[:], b_hid[:, :])])
    sc.wait("tensor", ld_bh.load[0], nbh)

    _, nqk = ld_qk.issue([(wqk_b[:, 128 * k:128 * (k + 1)],
                           w_qk[128 * k:128 * (k + 1), :]) for k in range(KD)])
    sc.wait("tensor", ld_qk.load[0], nqk)
    sc.wait("tensor", s_vec, v_p1)
    for strip in range(4):
        pslot = r_qkps.acquire("tensor")
        ev_mm = None
        for k in range(KD):
            tk = s_pe if k == KD - 1 else None
            e = sc.op("tensor", lambda h, ps=pslot, k=k, sp=strip: h.matmul(
                pf[4 + ps][:], wqk_b[:, 128 * k:128 * (k + 1)],
                xnT[:, S * k + 512 * sp: S * k + 512 * (sp + 1)],
                start=(k == 0), stop=(k == KD - 1)), tick=tk)
            ev_mm = e or ev_mm
        sc.wait_ev("scalar", ev_mm)
        ev_si = sc.op("scalar", lambda h, ps=pslot: h.activation(
            qk_bf[:], pf[4 + ps][:], AF.Silu, bias=bqk_sb[:]), tick=s_act)
        r_qkps.release(ev_si)
        sc.wait_ev("vector", ev_si)
        sc.op("vector", lambda h, sp=strip: h.tensor_scalar(
            qT[:, 512 * sp:512 * (sp + 1)], qk_bf[:], gq_sb[:], bq_sb[:], OP.mult, OP.add))
        sc.op("vector", lambda h, sp=strip: h.tensor_scalar(
            kT[:, 512 * sp:512 * (sp + 1)], qk_bf[:], gk_sb[:], bk_sb[:], OP.mult, OP.add),
            tick=s_vec)
    v_qk = s_vec.n

    mean_jobs = []
    for c in range(OGU // 128):
        for hh in range(2):
            mean_jobs.append((w_gu[128 * c:128 * (c + 1), 1024 * hh:1024 * (hh + 1)], 1024))
    NJ_GU = len(mean_jobs)
    for rb in range(KD):
        for c0 in range(0, FF, 1024):
            w = min(1024, FF - c0)
            mean_jobs.append((w_dn[128 * rb:128 * (rb + 1), c0:c0 + w], w))
    NMJ = len(mean_jobs)
    m_iss = [0]
    m_con = [0]

    def mean_issue_one():
        i = m_iss[0]
        src, w = mean_jobs[i]
        sc.op("scalar", lambda h, s=i % 3, src=src, w=w: h.dma_start(
            wmean_b[s][:, :w], src), tick=s_win[i % 3], inc=16)
        m_iss[0] += 1

    def mean_pump(k):
        for _ in range(k):
            if m_con[0] >= NMJ:
                return
            while m_iss[0] < NMJ and m_iss[0] < m_con[0] + 3:
                mean_issue_one()
            i = m_con[0]
            w = mean_jobs[i][1]
            sc.wait("scalar", s_win[i % 3], 16 * (i // 3 + 1))
            sc.op("scalar", lambda h, s=i % 3, w=w, i=i: h.activation(
                trashw[:, :w], wmean_b[s][:, :w], AF.Abs, accum_out=abscols[:, i:i + 1]),
                tick=s_act)
            m_con[0] += 1

    for strip in range(HID2 // 512):
        wnext = ld_wh.issued % 2
        wslot, lw = ld_wh.issue([(whb[wnext][:, 512 * k:512 * (k + 1)],
                                  w_hid[128 * k:128 * (k + 1), 512 * strip:512 * (strip + 1)])
                                 for k in range(KD)])
        us_hid.acquire("scalar")
        sc.wait("tensor", ld_wh.load[wslot], lw)
        for t in range(T):
            pslot = r_hps.acquire("tensor")
            sc.op("tensor", lambda h, ps=pslot, sp=strip: h.matmul(
                pf[ps][:], onesb_sb[:], bhid_row[:, 512 * sp:512 * (sp + 1)],
                start=True, stop=False))
            ev_mm = None
            for k in range(KD):
                tk = s_pe if k == KD - 1 else None
                e = sc.op("tensor", lambda h, ps=pslot, k=k, t=t, ws=wslot: h.matmul(
                    pf[ps][:], xnT[:, S * k + 128 * t: S * k + 128 * (t + 1)],
                    whb[ws][:, 512 * k:512 * (k + 1)],
                    start=False, stop=(k == KD - 1)), tick=tk)
                ev_mm = e or ev_mm
            if t == T - 1:
                ld_wh.release(ev_mm)
            sc.wait_ev("scalar", ev_mm)
            ev_si = sc.op("scalar", lambda h, ps=pslot, t=t: h.activation(
                hstg[:, 512 * t:512 * (t + 1)], pf[ps][:], AF.Silu), tick=s_act)
            r_hps.release(ev_si)
        sc.wait("sync", s_act, s_act.n)
        dsth = hbm_pkc(hid_d, 0, T, 512 * strip, 512 * (strip + 1))
        srch = sb_pkc(hstg, T, 0, T, 0, 512)
        us_hid.issue(dsth, srch)
    a_p2 = s_act.n

    es_b.close()
    es_r1.close()

    es_r2 = ExitStack()
    acc = es_r2.enter_context(nc.sbuf_tensor("acc", [128, T * D], BF16))

    es_c = ExitStack()
    sbc = lambda nm, shape, dt: es_c.enter_context(nc.sbuf_tensor(nm, shape, dt))
    bout_row = sbc("bout_row", [1, D], BF16)
    vq_sb = [sbc("vq_sb%d" % i, [128, T * 1024], BF16) for i in range(2)]
    gq_t = [sbc("gq_t%d" % i, [128, 1024], BF16) for i in range(2)]
    woq_sb = sbc("woq_sb", [128, 8 * D], BF16)
    wmean_b = [sbc("wmean%d" % i, [128, 1024], F32) for i in range(3)]
    a_stage = sbc("a_stage", [128, 1024], BF16)
    a_tmp = [sbc("a_tmp%d" % i, [128, 512], BF16) for i in range(2)]
    avg_stage = sbc("avg_stage", [128, 1024], BF16)
    avgT_stage = sbc("avgT_stage", [128, 1024], BF16)

    fence("gpsimd", "sync", "vector")
    us_hid.wait_all("vector")

    ld_bo = DStream("ldbo", 1, "gpsimd")
    ld_v = DStream("ldv", 2, "gpsimd")
    ld_g = DStream("ldg", 2, "gpsimd")
    ld_wo = DStream("ldwo", 2, "gpsimd")
    r_sim = Ring(2)
    r_av = Ring(1)
    r_out = Ring(1)
    r_ast = Ring(2)
    r_atmp = Ring(2)
    r_avgs = Ring(1)
    r_avT = Ring(1)

    _, nbo = ld_bo.issue([(bout_row[:], b_out[:, :])])
    sc.wait("tensor", ld_bo.load[0], nbo)
    sc.wait("tensor", s_act, a_p2)

    us_hid.wait_all("gpsimd")
    ev_wo_last = None
    vslot0, lv0 = ld_v.issue([(vq_sb[0][:, 1024 * t:1024 * (t + 1)],
                               hid_d[128 * t:128 * (t + 1), 0:1024]) for t in range(T)])
    vload = {0: (vslot0, lv0)}
    for hp in range(4):
        vslot, lv = vload[hp]
        lwo_h = []
        for dh in range(2):
            _, lw_ = ld_wo.issue([(woq_sb[:, D * k + 1024 * dh:D * k + 1024 * (dh + 1)],
                                   w_out[1024 * hp + 128 * k:1024 * hp + 128 * (k + 1),
                                         1024 * dh:1024 * (dh + 1)])
                                  for k in range(8)])
            lwo_h.append(lw_)
        if hp + 1 < 4:
            vload[hp + 1] = ld_v.issue(
                [(vq_sb[(hp + 1) % 2][:, 1024 * t:1024 * (t + 1)],
                  hid_d[128 * t:128 * (t + 1), 1024 * (hp + 1):1024 * (hp + 2)])
                 for t in range(T)])
        if hp == 0:
            sc.wait("tensor", s_vec, v_qk)
        sim_evs = {}
        sim_slots = {}

        def emit_sims(t, g, hp=hp, vslot=vslot, lv=lv):
            pslot = r_sim.acquire("tensor")
            if t == 0 and g == 0:
                sc.wait("tensor", ld_v.load[vslot], lv)
            ev_sim = None
            for j in range(4):
                kt = 4 * g + j
                tk = s_pe if j == 3 else None
                e = sc.op("tensor", lambda h, ps=pslot, kt=kt, j=j, t=t: h.matmul(
                    pf[ps][:, 128 * j:128 * (j + 1)],
                    kT[:, 128 * kt:128 * (kt + 1)],
                    qT[:, 128 * t:128 * (t + 1)],
                    start=True, stop=True), tick=tk)
                ev_sim = e or ev_sim
            sim_evs[(t, g)] = ev_sim
            sim_slots[(t, g)] = pslot

        emit_sims(0, 0)
        emit_sims(0, 1)
        for t in range(T):
            gslot, lg = ld_g.issue([(gq_t[ld_g.issued % 2][:],
                                     hid_d[128 * t:128 * (t + 1),
                                           HV + 1024 * hp:HV + 1024 * (hp + 1)])])
            r_av.acquire("tensor")
            ev_av = None
            for g in range(4):
                pslot = sim_slots.pop((t, g))
                aslot = r_ast.acquire("vector")
                atslot = r_atmp.acquire("scalar")
                sc.wait_ev("scalar", sim_evs.pop((t, g)))
                ev_relu = sc.op("scalar", lambda h, ps=pslot, ats=atslot: h.activation(
                    a_tmp[ats][:], pf[ps][:], AF.Relu), tick=s_act)
                r_sim.release(ev_relu)
                sc.wait_ev("vector", ev_relu)
                ev_a = sc.op("vector", lambda h, ats=atslot, a=aslot: h.tensor_tensor(
                    a_stage[:, 512 * a:512 * (a + 1)], a_tmp[ats][:], a_tmp[ats][:],
                    OP.mult), tick=s_vec)
                r_atmp.release(ev_a)
                if g + 2 < 4:
                    emit_sims(t, g + 2)
                elif t + 1 < T:
                    emit_sims(t + 1, g - 2)
                sc.wait_ev("tensor", ev_a)
                for j in range(4):
                    kt = 4 * g + j
                    for hf in range(2):
                        tk = s_pe if (j == 3 and hf == 1) else None
                        e = sc.op("tensor", lambda h, j=j, kt=kt, hf=hf, a=aslot, vs=vslot: h.matmul(
                            pf[2 + hf][:],
                            a_stage[:, 512 * a + 128 * j:512 * a + 128 * (j + 1)],
                            vq_sb[vs][:, 1024 * kt + 512 * hf:1024 * kt + 512 * (hf + 1)],
                            start=(kt == 0), stop=(kt == T - 1)), tick=tk)
                        ev_av = e or ev_av
                r_ast.release(ev_av)
            mean_pump(5)
            r_avgs.acquire("vector")
            sc.wait_ev("vector", ev_av)
            sc.wait("vector", ld_g.load[gslot], lg)
            if hp == 0 and t == 0:
                sc.wait("vector", s_mem0, 1)
            r_avT.acquire("tensor")
            ev_aT_h = []
            ev_avg = None
            for hf in range(2):
                e_avg = sc.op("vector", lambda h, hf=hf, gs=gslot: h.scalar_tensor_tensor(
                    avg_stage[:, 512 * hf:512 * (hf + 1)], pf[2 + hf][:], invs2_t[:],
                    gq_t[gs][:, 512 * hf:512 * (hf + 1)],
                    OP.mult, OP.mult), tick=s_vec)
                ev_avg = e_avg
                ps_h = r_tp.acquire("tensor")
                sc.wait_ev("tensor", e_avg)
                ev_tp = None
                for j in range(4):
                    jj = 4 * hf + j
                    tk = s_pe if j == 3 else None
                    e = sc.op("tensor", lambda h, ps=ps_h, j=j, jj=jj: h.transpose(
                        pb[ps][:, 128 * j:128 * (j + 1)], avg_stage[:, 128 * jj:128 * (jj + 1)],
                        id_sb[:]), tick=tk)
                    ev_tp = e or ev_tp
                if hf == 1:
                    r_avgs.release(ev_tp)
                sc.wait_ev("vector", ev_tp)
                e_aT = sc.op("vector", lambda h, ps=ps_h, hf=hf: h.tensor_copy(
                    avgT_stage[:, 512 * hf:512 * (hf + 1)], pb[ps][:, 0:512]), tick=s_vec)
                ev_aT_h.append(e_aT)
                r_tp.release(e_aT)
            r_av.release(ev_avg)
            ld_g.release(ev_avg)
            for dh in range(2):
                sc.wait("tensor", ld_wo.load[dh], lwo_h[dh])
                r_out.acquire("tensor")
                ev_wo = None
                if hp == 0:
                    for j2 in range(2):
                        sc.op("tensor", lambda h, dh=dh, j2=j2: h.matmul(
                            pf[4 + j2][:], onesb_sb[:],
                            bout_row[:, 1024 * dh + 512 * j2:1024 * dh + 512 * (j2 + 1)],
                            start=True, stop=False))
                for hc in range(8):
                    if dh == 0 and hc == 0:
                        sc.wait_ev("tensor", ev_aT_h[0])
                    elif dh == 0 and hc == 4:
                        sc.wait_ev("tensor", ev_aT_h[1])
                    for j2 in range(2):
                        tk = s_pe if (hc == 7 and j2 == 1) else None
                        e = sc.op("tensor", lambda h, dh=dh, hc=hc, j2=j2, hp=hp: h.matmul(
                            pf[4 + j2][:],
                            avgT_stage[:, 128 * hc:128 * (hc + 1)],
                            woq_sb[:, D * hc + 1024 * dh + 512 * j2:
                                   D * hc + 1024 * dh + 512 * (j2 + 1)],
                            start=(hc == 0 and hp != 0), stop=(hc == 7)), tick=tk)
                        ev_wo = e or ev_wo
                if dh == 1:
                    r_avT.release(ev_wo)
                if t == T - 1:
                    ld_wo.release(ev_wo)
                    if dh == 1 and hp == 3:
                        ev_wo_last = ev_wo
                sc.wait_ev("vector", ev_wo)
                ev_ac = None
                for j2 in range(2):
                    dsta = acc[:, D * t + 1024 * dh + 512 * j2: D * t + 1024 * dh + 512 * (j2 + 1)]
                    tk = s_vec if j2 == 1 else None
                    if hp == 0:
                        e = sc.op("vector", lambda h, d=dsta, j2=j2: h.tensor_copy(
                            d, pf[4 + j2][:]), tick=tk)
                    else:
                        e = sc.op("vector", lambda h, d=dsta, j2=j2: h.tensor_tensor(
                            d, pf[4 + j2][:], d, OP.add), tick=tk)
                    ev_ac = e or ev_ac
                r_out.release(ev_ac)
            if t == T - 1:
                ld_v.release(ev_av)
    v_p3 = s_vec.n

    es_c.close()

    mean_pump(NMJ)
    a_means = s_act.n
    sc.wait("vector", s_act, a_means)
    V2 = lambda fn, **kw: sc.op("vector", fn, **kw)
    SP2 = lambda **kw: sc.op("vector", lambda h: h.tensor_copy(
        spacer_t[:, 0:512], spacer_t[:, 512:1024]), **kw)
    V2(lambda h: h.tensor_reduce(red_tmp[:, 0:1], abscols[:, 0:NJ_GU], AX.X, OP.add))
    V2(lambda h, a=NJ_GU, b=len(mean_jobs): h.tensor_reduce(
        red_tmp[:, 1:2], abscols[:, a:b], AX.X, OP.add))
    ev_red = SP2(tick=s_vec)
    sc.wait_ev("tensor", ev_red)
    sc.wait_ev("tensor", ev_wo_last)
    sc.wait("tensor", s_vec, v_p3)
    sc.op("tensor", lambda h: h.matmul(pf[0][0:1, 0:1], red_tmp[:, 0:1], onesf_sb[:],
                                       start=True, stop=True))
    ev_ps = sc.op("tensor", lambda h: h.matmul(pf[0][0:1, 1:2], red_tmp[:, 1:2], onesf_sb[:],
                                               start=True, stop=True), tick=s_pe)
    sc.wait_ev("vector", ev_ps)
    V2(lambda h: h.tensor_copy(cm_row[:, :], pf[0][0:1, 0:2]))
    SP2()
    V2(lambda h: h.tensor_scalar(cm_row[:, 0:1], cm_row[:, 0:1], 1.0 / (OGU * D), None, OP.mult))
    V2(lambda h: h.tensor_scalar(cm_row[:, 1:2], cm_row[:, 1:2], 1.0 / (D * FF), None, OP.mult))
    SP2()
    V2(lambda h: h.tensor_scalar(cm_row[:, :], cm_row[:, :], 1e-5, None, OP.max))
    ev_cm = SP2(tick=s_vec)
    sc.wait_ev("tensor", ev_cm)
    ev_bc = sc.op("tensor", lambda h: h.matmul(pf[1][:, 0:2], onesrf_sb[:], cm_row[:, :],
                                               start=True, stop=True), tick=s_pe)
    sc.wait_ev("vector", ev_bc)
    V2(lambda h: h.tensor_copy(cmb[:, :], pf[1][:, 0:2]))
    SP2()
    V2(lambda h: h.tensor_scalar(thr_gu[:], cmb[:, 0:1], 0.5, None, OP.mult))
    V2(lambda h: h.tensor_scalar(nthr_gu[:], cmb[:, 0:1], -0.5, None, OP.mult))
    V2(lambda h: h.tensor_scalar(ncm_gu[:], cmb[:, 0:1], -1.0, None, OP.mult))
    V2(lambda h: h.tensor_scalar(thr_dn[:], cmb[:, 1:2], 0.5, None, OP.mult))
    V2(lambda h: h.tensor_scalar(nthr_dn[:], cmb[:, 1:2], -0.5, None, OP.mult))
    ev_thr = V2(lambda h: h.tensor_scalar(ncm_dn[:], cmb[:, 1:2], -1.0, None, OP.mult),
                tick=s_vec)

    es_d = ExitStack()
    sbd = lambda nm, shape, dt: es_d.enter_context(nc.sbuf_tensor(nm, shape, dt))
    xqT_sb = sbd("xqT_sb", [128, KD * S], BF16)
    wqT_g = [sbd("wqT_g%d" % i, [128, KD * 256], BF16) for i in range(2)]
    wqT_u = [sbd("wqT_u%d" % i, [128, KD * 256], BF16) for i in range(2)]
    wg_in = [sbd("wg_in%d" % i, [128, D], F32) for i in range(1)]
    c1_bf = sbd("c1_bf", [128, D], BF16)
    wqn_bf = [sbd("wqn%d" % i, [128, D], BF16) for i in range(2)]
    g_blk = sbd("g_blk", [128, 256], BF16)
    gup_blk = sbd("gup_blk", [128, 256], BF16)
    h2stg = sbd("h2stg", [128, T * 256], BF16)

    fence("gpsimd", "sync")

    ld_xqT = DStream("ldxqT", 1, "gpsimd")
    ld_wg = DStream("ldwg", 1, "gpsimd")
    us_h2 = UStore("us_h2", 1)
    r_wqn = Ring(2)
    r_gups = Ring(2)
    r_gblk = Ring(1)

    sc.wait_ev("vector", ev_thr)
    sc.op("vector", lambda h: h.tensor_copy(spacer_t[:, 0:512], spacer_t[:, 512:1024]))
    ev_sg = sc.op("vector", lambda h: h.tensor_scalar(
        sg_all[:], rs1_all[:], ncm_gu[:], None, OP.mult), tick=s_vec)
    us_xq.wait_all("gpsimd")
    _, lxq = ld_xqT.issue([(sb_pkc(xqT_sb, KD, 0, KD, 0, S),
                            hbm_pkc(xqT_d, 0, KD, 0, S))])
    sc.wait("tensor", ld_xqT.load[0], lxq)
    sc.wait_ev("tensor", ev_thr)

    NP = OGU // 512
    r_wqT = Ring(2)
    build_done = {}
    pend_q = {}

    def emit_build_quant(jp, which, c, first=False):
        base = 256 * jp + (FF if which else 0)
        wslot, lwg = ld_wg.issue([(wg_in[0][:],
                                   w_gu[base + 128 * c: base + 128 * (c + 1), :])])
        qslot = r_wqn.acquire("vector")
        sc.wait("vector", ld_wg.load[wslot], lwg)
        if first:
            sc.wait_ev("vector", ev_thr)
        sc.op("vector", lambda h, ws=wslot: h.tensor_scalar(
            c1_bf[:], wg_in[ws][:], thr_gu[:], None, OP.is_gt))
        ev_q = sc.op("vector", lambda h, ws=wslot, qs=qslot: h.scalar_tensor_tensor(
            wqn_bf[qs][:], wg_in[ws][:], nthr_gu[:], c1_bf[:], OP.is_lt, OP.subtract),
            tick=s_vec)
        ld_wg.release(ev_q)
        pend_q[(jp, which, c)] = (qslot, ev_q)

    def emit_build_tp(jp, which, c, buf):
        qslot, ev_q = pend_q.pop((jp, which, c))
        sc.wait_ev("tensor", ev_q)
        wqT_t = (wqT_g if which == 0 else wqT_u)[buf]
        for half in range(2):
            ps_ = r_tp.acquire("tensor")
            ev_tp = None
            for jj in range(8):
                kk = 8 * half + jj
                tk = s_pe if jj == 7 else None
                e = sc.op("tensor", lambda h, qs=qslot, ps=ps_, jj=jj, kk=kk: h.transpose(
                    pb[ps][:, 128 * jj:128 * (jj + 1)],
                    wqn_bf[qs][:, 128 * kk:128 * (kk + 1)], id_sb[:]), tick=tk)
                ev_tp = e or ev_tp
            if half == 1:
                r_wqn.release(ev_tp)
            sc.wait_ev("vector", ev_tp)
            dstw = sb_pkc(wqT_t, KD, 8 * half, 8 * half + 8, 128 * c, 128 * (c + 1))
            srcw = pb[ps_].rearrange("p (k c2) -> p k c2", k=8)
            ev_ev = sc.op("vector", lambda h, d=dstw, s2=srcw: h.tensor_copy(d, s2),
                          tick=s_vec)
            r_tp.release(ev_ev)

    def emit_build_chunk(jp, which, c, buf, first=False):
        emit_build_quant(jp, which, c, first=first)
        emit_build_tp(jp, which, c, buf)

    r_wqT.acquire("vector")
    for which in (0, 1):
        for c in (0, 1):
            emit_build_chunk(0, which, c, 0, first=(which == 0 and c == 0))
    build_done[0] = s_vec.n

    for jp in range(NP):
        buf = jp % 2
        sc.wait("tensor", s_vec, build_done[jp])
        r_gblk.acquire("vector")
        us_h2.acquire("vector")
        chunks = [(w, c) for w in (0, 1) for c in (0, 1)] if jp + 1 < NP else []
        qchunks = list(chunks)
        if chunks:
            r_wqT.acquire("vector")
        ev_h2 = None
        ev_gu_last = None
        for t in range(T):
            pp = r_gups.acquire("tensor")
            for k in range(KD):
                sc.op("tensor", lambda h, pp=pp, k=k, t=t, b=buf: h.matmul(
                    pf[2 * pp][:, 0:256], xqT_sb[:, S * k + 128 * t: S * k + 128 * (t + 1)],
                    wqT_g[b][:, 256 * k:256 * (k + 1)],
                    start=(k == 0), stop=(k == KD - 1)))
            ev_gu = None
            for k in range(KD):
                tk = s_pe if k == KD - 1 else None
                e = sc.op("tensor", lambda h, pp=pp, k=k, t=t, b=buf: h.matmul(
                    pf[2 * pp + 1][:, 0:256], xqT_sb[:, S * k + 128 * t: S * k + 128 * (t + 1)],
                    wqT_u[b][:, 256 * k:256 * (k + 1)],
                    start=(k == 0), stop=(k == KD - 1)), tick=tk)
                ev_gu = e or ev_gu
            ev_gu_last = ev_gu
            sc.wait_ev("vector", ev_gu)
            V3 = lambda fn, **kw: sc.op("vector", fn, **kw)
            V3(lambda h, pp=pp, t=t: h.tensor_scalar(
                g_blk[:], pf[2 * pp][:, 0:256], sg_all[:, t:t + 1], None, OP.mult))
            ev_gup = V3(lambda h, pp=pp, t=t: h.scalar_tensor_tensor(
                gup_blk[:], pf[2 * pp + 1][:, 0:256], sg_all[:, t:t + 1], g_blk[:],
                OP.mult, OP.mult), tick=s_vec)
            r_gups.release(ev_gup)
            ev_h2 = V3(lambda h, t=t: h.scalar_tensor_tensor(
                h2stg[:, 256 * t:256 * (t + 1)], g_blk[:], 0.0, gup_blk[:], OP.max, OP.mult),
                tick=s_vec)
            if qchunks and t in (1, 4, 7, 10):
                emit_build_quant(jp + 1, *qchunks.pop(0))
            if chunks and t in (3, 6, 9, 12):
                emit_build_tp(jp + 1, *chunks.pop(0), 1 - buf)
        build_done[jp + 1] = s_vec.n
        r_wqT.release(ev_gu_last)
        r_gblk.release(ev_h2)
        sc.wait("sync", s_vec, s_vec.n)
        dsth2 = hbm_pkc(h2_d, 0, T, 256 * jp, 256 * (jp + 1))
        srch2 = sb_pkc(h2stg, T, 0, T, 0, 256)
        us_h2.issue(dsth2, srch2)

    es_d.close()

    es_d2 = ExitStack()
    sbd2 = lambda nm, shape, dt: es_d2.enter_context(nc.sbuf_tensor(nm, shape, dt))
    h2row = [sbd2("h2row%d" % i, [128, FF], BF16) for i in range(2)]
    y2_f = sbd2("y2_f", [128, FF], F32)
    hq_bf = sbd2("hq_bf", [128, FF], BF16)
    hqst = [sbd2("hqst%d" % i, [128, KF * 128], BF16) for i in range(2)]

    fence("gpsimd", "sync", "vector", "scalar")
    us_h2.wait_all("vector")
    us_h2.wait_all("scalar")

    ld_h2 = DStream("ldh2", 2, "gpsimd")
    us_hq = UStore("us_hq", 2)
    r_y2 = Ring(1)
    r_hq = Ring(1)

    us_h2.wait_all("gpsimd")
    sfac2_all = st
    V4 = lambda fn, **kw: sc.op("vector", fn, **kw)
    SP4 = lambda **kw: sc.op("vector", lambda h: h.tensor_copy(
        spacer_t[:, 0:512], spacer_t[:, 512:1024]), **kw)
    evA = {}
    hslots = {}

    def emit_A_load(t):
        hnext = ld_h2.issued % 2
        hslot, lh2 = ld_h2.issue([(h2row[hnext][:], h2_d[128 * t:128 * (t + 1), :])])
        hslots[t] = (hslot, lh2)

    def emit_A_chain(t):
        hslot, lh2 = hslots[t]
        sc.wait("vector", ld_h2.load[hslot], lh2)
        V4(lambda h, s=hslot: h.tensor_reduce(
            st["mx2"][:], h2row[s][:], AX.X, OP.max, apply_absolute_value=True))
        SP4()
        V4(lambda h: h.tensor_scalar(st["cmx2"][:], st["mx2"][:], 1e-5, None, OP.max))
        SP4()
        V4(lambda h, t=t: h.tensor_scalar(rs2_all[:, t:t + 1], st["cmx2"][:], 1.0 / 127, None, OP.mult))
        SP4()
        V4(lambda h: h.reciprocal(st["icx2"][:], st["cmx2"][:]))
        SP4()
        V4(lambda h, t=t: h.tensor_scalar(sf2_all[:, t:t + 1], st["icx2"][:], 127.0, None, OP.mult))
        evA[t] = SP4(tick=s_vec)

    emit_A_load(0)
    emit_A_chain(0)
    for t in range(T):
        if t + 1 < T:
            emit_A_load(t + 1)
        hslot = hslots[t][0]
        r_y2.acquire("scalar")
        sc.wait_ev("scalar", evA.pop(t))
        ev_y2_h = []
        for hh in range(2):
            e = sc.op("scalar", lambda h, s=hslot, t=t, hh=hh: h.activation(
                y2_f[:, 2816 * hh:2816 * (hh + 1)], h2row[s][:, 2816 * hh:2816 * (hh + 1)],
                AF.Identity, bias=c_rne_t[:], scale=sf2_all[:, t:t + 1]),
                tick=s_act)
            ev_y2_h.append(e)
        ld_h2.release(ev_y2_h[1])
        r_hq.acquire("vector")
        qslot = us_hq.acquire("scalar")
        ev_hq = None
        for hh in range(2):
            sc.wait_ev("vector", ev_y2_h[hh])
            ev_hq = V4(lambda h, hh=hh: h.tensor_scalar(
                hq_bf[:, 2816 * hh:2816 * (hh + 1)], y2_f[:, 2816 * hh:2816 * (hh + 1)],
                C_RNE, None, OP.subtract), tick=s_vec)
            base = 22 * hh
            done = 0
            sc.wait_ev("tensor", ev_hq)
            while done < 22:
                cnt = min(8, 22 - done)
                ps_ = r_tp.acquire("tensor")
                ev_tp = None
                for jj in range(cnt):
                    kk = base + done + jj
                    tk = s_pe if jj == cnt - 1 else None
                    e = sc.op("tensor", lambda h, ps=ps_, jj=jj, kk=kk: h.transpose(
                        pb[ps][:, 128 * jj:128 * (jj + 1)], hq_bf[:, 128 * kk:128 * (kk + 1)],
                        id_sb[:]), tick=tk)
                    ev_tp = e or ev_tp
                if hh == 1 and done + cnt >= 22:
                    r_hq.release(ev_tp)
                sc.wait_ev("scalar", ev_tp)
                dsts = hqst[qslot][:, 128 * (base + done):128 * (base + done + cnt)]
                ev_ev = sc.op("scalar", lambda h, ps=ps_, d=dsts, c=cnt: h.activation(
                    d, pb[ps][:, 0:128 * c], AF.Identity), tick=s_act)
                r_tp.release(ev_ev)
                done += cnt
        r_y2.release(ev_hq)
        if t + 1 < T:
            emit_A_chain(t + 1)
        sc.wait("sync", s_act, s_act.n)
        dsthq = hqT_d[t].rearrange("(k p) c -> p k c", p=128)
        srchq = sb_pkc(hqst[qslot], KF, 0, KF, 0, 128)
        us_hq.issue(dsthq, srchq)

    es_d2.close()

    es_e = ExitStack()
    sbe = lambda nm, shape, dt: es_e.enter_context(nc.sbuf_tensor(nm, shape, dt))
    wdT_q = sbe("wdT_q", [128, KF * 512], BF16)
    wd_in = [sbe("wd_in%d" % i, [128, 2816], F32) for i in range(3)]
    wqnd_bf = [sbe("wqnd%d" % i, [128, 2816], BF16) for i in range(2)]
    hq_in = [sbe("hq_in%d" % i, [128, KF * 128], BF16) for i in range(2)]
    x5_in = [sbe("x5_in%d" % i, [128, 512], F32) for i in range(2)]
    outst = [sbe("outst%d" % i, [128, 512], F32) for i in range(2)]
    ffnst = [sbe("ffnst%d" % i, [128, 512], F32) for i in range(2)]

    fence("gpsimd", "sync", "vector", "scalar")
    us_hq.wait_all("gpsimd")
    us_hq.wait_all("vector")
    us_hq.wait_all("scalar")

    ld_wd = DStream("ldwd", 3, "gpsimd")
    ld_hq = DStream("ldhq", 2, "sync")
    ld_x5 = DStream("ldx5", 2, "gpsimd")
    us_out = UStore("us_out", 2, qeng="scalar")
    r_wqnd = Ring(2)
    r_dnps = Ring(2)
    r_wdT = Ring(1)

    us_hq.wait_all("sync")
    sc.op("vector", lambda h: h.tensor_copy(spacer_t[:, 0:512], spacer_t[:, 512:1024]))
    sc.op("vector", lambda h: h.tensor_scalar(
        sd_all[:], rs2_all[:], ncm_dn[:], None, OP.mult))
    for q in range(4):
        r_wdT.acquire("vector")
        for c in range(4):
            for hh in range(2):
                wnext = ld_wd.issued % 3
                wslot, lwd = ld_wd.issue([(wd_in[wnext][:],
                                           w_dn[512 * q + 128 * c:512 * q + 128 * (c + 1),
                                                2816 * hh:2816 * (hh + 1)])])
                qslot = r_wqnd.acquire("vector")
                sc.wait("vector", ld_wd.load[wslot], lwd)
                sc.op("vector", lambda h, ws=wslot, qs=qslot: h.tensor_scalar(
                    wqnd_bf[qs][:], wd_in[ws][:], thr_dn[:], None, OP.is_gt))
                ev_qd = sc.op("vector", lambda h, ws=wslot, qs=qslot: h.scalar_tensor_tensor(
                    wqnd_bf[qs][:], wd_in[ws][:], nthr_dn[:], wqnd_bf[qs][:], OP.is_lt, OP.subtract),
                    tick=s_vec)
                ld_wd.release(ev_qd)
                sc.wait_ev("tensor", ev_qd)
                done = 0
                while done < 22:
                    cnt = min(8, 22 - done)
                    ps_ = r_tp.acquire("tensor")
                    ev_tp = None
                    for jj in range(cnt):
                        kk = done + jj
                        tk = s_pe if jj == cnt - 1 else None
                        e = sc.op("tensor", lambda h, qs=qslot, ps=ps_, jj=jj, kk=kk: h.transpose(
                            pb[ps][:, 128 * jj:128 * (jj + 1)],
                            wqnd_bf[qs][:, 128 * kk:128 * (kk + 1)], id_sb[:]), tick=tk)
                        ev_tp = e or ev_tp
                    if done + cnt >= 22:
                        r_wqnd.release(ev_tp)
                    sc.wait_ev("scalar", ev_tp)
                    dstw = sb_pkc(wdT_q, KF, 22 * hh + done, 22 * hh + done + cnt,
                                  128 * c, 128 * (c + 1))
                    srcw = pb[ps_].rearrange("p (k c2) -> p k c2", k=8)[:, 0:cnt]
                    ev_ev = sc.op("scalar", lambda h, d=dstw, s=srcw: h.activation(
                        d, s, AF.Identity), tick=s_act)
                    r_tp.release(ev_ev)
                    done += cnt
        v_wdT = s_act.n
        sc.wait("tensor", s_act, v_wdT)
        ev_dn = None
        for t in range(T):
            hnext = ld_hq.issued % 2
            hslot, lhq = ld_hq.issue([(hq_in[hnext][:],
                                       hqT_d[t].rearrange("(k p) c -> p k c", p=128))])
            xnext = ld_x5.issued % 2
            xslot, lx5 = ld_x5.issue([(x5_in[xnext][:],
                                       x_in[128 * t:128 * (t + 1), 512 * q:512 * (q + 1)])])
            pslot = r_dnps.acquire("tensor")
            sc.wait("tensor", ld_hq.load[hslot], lhq)
            ev_dn = None
            for hk in range(KF):
                tk = s_pe if hk == KF - 1 else None
                e = sc.op("tensor", lambda h, ps=pslot, hk=hk, hs=hslot: h.matmul(
                    pf[ps][:], hq_in[hs][:, 128 * hk:128 * (hk + 1)],
                    wdT_q[:, 512 * hk:512 * (hk + 1)],
                    start=(hk == 0), stop=(hk == KF - 1)), tick=tk)
                ev_dn = e or ev_dn
            ld_hq.release(ev_dn)
            oslot = us_out.acquire("vector")
            sc.wait_ev("scalar", ev_dn)
            ev_f = sc.op("scalar", lambda h, ps=pslot, os=oslot, t=t: h.activation(
                ffnst[os][:], pf[ps][:], AF.Identity, scale=sd_all[:, t:t + 1]),
                tick=s_act)
            r_dnps.release(ev_f)
            sc.wait("vector", ld_x5.load[xslot], lx5)
            V5 = lambda fn, **kw: sc.op("vector", fn, **kw)
            ev_e1 = V5(lambda h, xs=xslot, os=oslot, t=t, q=q: h.scalar_tensor_tensor(
                outst[os][:], x5_in[xs][:], rinv1_all[:, t:t + 1],
                acc[:, D * t + 512 * q: D * t + 512 * (q + 1)], OP.mult, OP.add), tick=s_vec)
            ld_x5.release(ev_e1)
            ev_e2 = V5(lambda h, os=oslot, t=t: h.tensor_scalar(
                outst[os][:], outst[os][:], negmr_all[:, t:t + 1], None, OP.add), tick=s_vec)
            sc.wait_ev("vector", ev_f)
            ev_e3 = V5(lambda h, os=oslot: h.tensor_tensor(
                outst[os][:], outst[os][:], ffnst[os][:], OP.add), tick=s_vec)
            sc.wait_ev("scalar", ev_e3)
            us_out.issue(out[128 * t:128 * (t + 1), 512 * q:512 * (q + 1)], outst[oslot][:])
            if debug:
                sc.wait_ev("sync", ev_f)
                sc.op("sync", lambda h, os=oslot, t=t, q=q: h.dma_start(
                    dbg_ffn[128 * t:128 * (t + 1), 512 * q:512 * (q + 1)], ffnst[os][:]),
                    tick=us_out.st[oslot], inc=16)
        r_wdT.release(ev_dn)

    if debug:
        s_dbg = ESem("s_dbg")
        sc.wait("sync", s_vec, s_vec.n)
        sc.wait("sync", s_act, s_act.n)
        for dd, ss in [(dbg_p1, dbg1_sb[:]), (dbg_acc, acc[:]), (dbg_qT, qT[:]), (dbg_kT, kT[:]),
                       (dbg_st[:, 0:T], rinv1_all[:]), (dbg_st[:, T:2 * T], negmr_all[:]),
                       (dbg_st[:, 2 * T:3 * T], rs1_all[:]), (dbg_st[:, 3 * T:4 * T], rs2_all[:]),
                       (dbg_st[:, 4 * T:4 * T + 2], cmb[:]),
                       (dbg_st[:, 4 * T + 2:4 * T + 3], thr_gu[:]),
                       (dbg_st[:, 4 * T + 3:4 * T + 4], thr_dn[:])]:
            sc.op("sync", lambda h, d=dd, s2=ss: h.dma_start(d, s2, allow_slow_non_contiguous=True), tick=s_dbg, inc=16)
        sc.wait("sync", s_dbg, s_dbg.n)

    us_out.wait_all("scalar")

    es_e.close()
    es_r2.close()

    block = es_p.enter_context(nc.Block())
    for eng in ENGINES:
        thunks = sc.q[eng]
        if not thunks:
            continue
        getattr(block, eng)(lambda h, th=thunks: [f(h) for f in th])
    es_p.close()
    return nc


_NC_CACHE = {}


def _get_nc(debug=False):
    key = bool(debug)
    if key not in _NC_CACHE:
        _NC_CACHE[key] = build_program(debug=key)
    return _NC_CACHE[key]


def kernel(L0_attn_x, L0_attn_W_hidden, L0_attn_b_hidden, L0_attn_W_qk,
           L0_attn_b_qk, L0_attn_gamma_q, L0_attn_beta_q, L0_attn_gamma_k,
           L0_attn_beta_k, L0_attn_W_out, L0_attn_b_out, L0_ffn_w_gate_up,
           L0_ffn_w_down, L0_ffn_norm_weight):
    x = np.ascontiguousarray(np.asarray(L0_attn_x, dtype=np.float32))
    com = {
        "w_hid": np.ascontiguousarray(np.asarray(L0_attn_W_hidden, dtype=np.float32)),
        "b_hid": np.asarray(L0_attn_b_hidden, dtype=np.float32).reshape(1, HID2),
        "w_qk": np.ascontiguousarray(np.asarray(L0_attn_W_qk, dtype=np.float32)),
        "b_qk": np.asarray(L0_attn_b_qk, dtype=np.float32).reshape(QKD, 1),
        "g_q": np.asarray(L0_attn_gamma_q, dtype=np.float32).reshape(QKD, 1),
        "be_q": np.asarray(L0_attn_beta_q, dtype=np.float32).reshape(QKD, 1),
        "g_k": np.asarray(L0_attn_gamma_k, dtype=np.float32).reshape(QKD, 1),
        "be_k": np.asarray(L0_attn_beta_k, dtype=np.float32).reshape(QKD, 1),
        "w_out": np.ascontiguousarray(np.asarray(L0_attn_W_out, dtype=np.float32)),
        "b_out": np.asarray(L0_attn_b_out, dtype=np.float32).reshape(1, D),
        "w_gu": np.ascontiguousarray(np.asarray(L0_ffn_w_gate_up, dtype=np.float32)),
        "w_dn": np.ascontiguousarray(np.asarray(L0_ffn_w_down, dtype=np.float32)),
        "nw": np.asarray(L0_ffn_norm_weight, dtype=np.float32).reshape(1, D),
        "ident": np.eye(128, dtype=ml_dtypes.bfloat16),
        "ones_bf": np.ones((1, 128), dtype=ml_dtypes.bfloat16),
        "ones_f": np.ones((128, 1), dtype=np.float32),
        "ones_rf": np.ones((1, 128), dtype=np.float32),
    }
    in_maps = [dict(com, x=np.ascontiguousarray(x[b])) for b in range(8)]
    nc = _get_nc(debug=False)
    res = run_bass_kernel_spmd(nc, in_maps, core_ids=list(range(8)))
    return np.stack([np.asarray(r["out"], dtype=np.float32) for r in res.results], axis=0)


# version 20
# speedup vs baseline: 1.0429x; 1.0088x over previous
import numpy as np
import ml_dtypes
from contextlib import ExitStack

import concourse.bass as bass
import concourse.mybir as mybir
from concourse.bass_utils import run_bass_kernel_spmd
from concourse.alu_op_type import AluOpType

F32 = mybir.dt.float32
BF16 = mybir.dt.bfloat16
AF = mybir.ActivationFunctionType
AX = mybir.AxisListType
OP = AluOpType

S = 2048
D = 2048
HID2 = 8192
HV = 4096
QKD = 128
FF = 5632
OGU = 2 * FF
T = S // 128
KD = D // 128
KF = FF // 128
EPS = 1e-5
C_RNE = float(np.float32(1.5 * 2 ** 23))
INV_S2 = 1.0 / float(S) ** 2

ENGINES = ("tensor", "vector", "scalar", "sync", "gpsimd")


def build_program(debug=False, paranoid=False):
    nc = bass.Bass("TRN2", target_bir_lowering=False, detect_race_conditions=False)

    x_in = nc.dram_tensor("x", [S, D], F32, kind="ExternalInput").ap()
    w_hid = nc.dram_tensor("w_hid", [D, HID2], F32, kind="ExternalInput").ap()
    b_hid = nc.dram_tensor("b_hid", [1, HID2], F32, kind="ExternalInput").ap()
    w_qk = nc.dram_tensor("w_qk", [D, QKD], F32, kind="ExternalInput").ap()
    b_qk = nc.dram_tensor("b_qk", [QKD, 1], F32, kind="ExternalInput").ap()
    g_q = nc.dram_tensor("g_q", [QKD, 1], F32, kind="ExternalInput").ap()
    be_q = nc.dram_tensor("be_q", [QKD, 1], F32, kind="ExternalInput").ap()
    g_k = nc.dram_tensor("g_k", [QKD, 1], F32, kind="ExternalInput").ap()
    be_k = nc.dram_tensor("be_k", [QKD, 1], F32, kind="ExternalInput").ap()
    w_out = nc.dram_tensor("w_out", [HV, D], F32, kind="ExternalInput").ap()
    b_out = nc.dram_tensor("b_out", [1, D], F32, kind="ExternalInput").ap()
    w_gu = nc.dram_tensor("w_gu", [OGU, D], F32, kind="ExternalInput").ap()
    w_dn = nc.dram_tensor("w_dn", [D, FF], F32, kind="ExternalInput").ap()
    nw = nc.dram_tensor("nw", [1, D], F32, kind="ExternalInput").ap()
    ident = nc.dram_tensor("ident", [128, 128], BF16, kind="ExternalInput").ap()
    ones_bf = nc.dram_tensor("ones_bf", [1, 128], BF16, kind="ExternalInput").ap()
    ones_f = nc.dram_tensor("ones_f", [128, 1], F32, kind="ExternalInput").ap()
    ones_rf = nc.dram_tensor("ones_rf", [1, 128], F32, kind="ExternalInput").ap()

    out = nc.dram_tensor("out", [S, D], F32, kind="ExternalOutput").ap()

    def scratch(name, shape, dt):
        if debug:
            return nc.dram_tensor(name, shape, dt, kind="ExternalOutput").ap()
        return nc.dram_tensor(name, shape, dt).ap()

    hid_d = scratch("hid_d", [S, HID2], BF16)
    xqT_d = scratch("xqT_d", [D, S], BF16)
    h2_d = scratch("h2_d", [S, FF], BF16)
    hqT_d = scratch("hqT_d", [T, FF, 128], BF16)
    if debug:
        dbg_ffn = nc.dram_tensor("dbg_ffn", [S, D], F32, kind="ExternalOutput").ap()
        dbg_p1 = nc.dram_tensor("dbg_p1", [128, 8 * T], F32, kind="ExternalOutput").ap()
        dbg_acc = nc.dram_tensor("dbg_acc", [128, T * D], BF16, kind="ExternalOutput").ap()
        dbg_qT = nc.dram_tensor("dbg_qT", [128, S], BF16, kind="ExternalOutput").ap()
        dbg_kT = nc.dram_tensor("dbg_kT", [128, S], BF16, kind="ExternalOutput").ap()
        dbg_st = nc.dram_tensor("dbg_st", [128, 4 * T + 8], F32, kind="ExternalOutput").ap()

    es_p = ExitStack()

    class ESem:
        def __init__(self, name):
            self.h = es_p.enter_context(nc.semaphore(name))
            self.n = 0

    class Sched:
        def __init__(self):
            self.q = {e: [] for e in ENGINES}
            self.psems = None

        def op(self, eng, fn, tick=None, inc=1):
            if self.psems is not None and eng in ("tensor", "vector", "scalar"):
                for oe, sem_ in self.psems.items():
                    if oe != eng and sem_.n > 0:
                        self.wait(eng, sem_, sem_.n)
                if tick is None:
                    tick = self.psems[eng]
                    inc = 1
            cap = None
            if tick is not None:
                tick.n += inc
                cap = (tick.h, inc)

            def thunk(h, fn=fn, cap=cap):
                ins = fn(h)
                if cap is not None:
                    ins.then_inc(cap[0], cap[1])
            self.q[eng].append(thunk)
            return (tick, tick.n) if tick is not None else None

        def wait(self, eng, sem, tgt):
            if tgt <= 0:
                return
            self.q[eng].append(lambda h, s=sem.h, t=tgt: h.wait_ge(s, t))

        def wait_ev(self, eng, ev):
            if ev is not None:
                self.wait(eng, ev[0], ev[1])

    sc = Sched()

    class Ring:
        def __init__(self, nslots):
            self.nslots = nslots
            self.acq = 0
            self.rels = []

        def acquire(self, eng):
            k = self.acq
            self.acq += 1
            if k >= self.nslots:
                idx = k - self.nslots
                assert idx < len(self.rels), "ring release missing (acq %d)" % k
                for ev in self.rels[idx]:
                    sc.wait_ev(eng, ev)
            return k % self.nslots

        def release(self, *evs):
            self.rels.append([e for e in evs if e is not None])

    class DStream:
        def __init__(self, name, nslots, qeng):
            self.load = [ESem(name + "_l%d" % i) for i in range(nslots)]
            self.nslots = nslots
            self.qeng = qeng
            self.issued = 0
            self.rels = []

        def issue(self, pairs):
            k = self.issued
            slot = k % self.nslots
            if k >= self.nslots:
                idx = k - self.nslots
                assert idx < len(self.rels), "dstream release missing (%d)" % k
                for ev in self.rels[idx]:
                    sc.wait_ev(self.qeng, ev)
            for o, i in pairs:
                sc.op(self.qeng, lambda h, o=o, i=i: h.dma_start(o, i),
                      tick=self.load[slot], inc=16)
            self.issued += 1
            return slot, self.load[slot].n

        def release(self, *evs):
            self.rels.append([e for e in evs if e is not None])

    class UStore:
        def __init__(self, name, nslots, qeng="sync"):
            self.st = [ESem(name + "_s%d" % i) for i in range(nslots)]
            self.nslots = nslots
            self.qeng = qeng
            self.n = 0

        def acquire(self, eng):
            k = self.n
            slot = k % self.nslots
            if k >= self.nslots:
                sc.wait(eng, self.st[slot], self.st[slot].n)
            return slot

        def issue(self, dst, src):
            slot = self.n % self.nslots
            sc.op(self.qeng, lambda h, d=dst, s=src: h.dma_start(d, s),
                  tick=self.st[slot], inc=16)
            self.n += 1

        def wait_all(self, eng):
            for s_ in self.st:
                sc.wait(eng, s_, s_.n)

    s_pe = ESem("s_pe")
    s_vec = ESem("s_vec")
    s_act = ESem("s_act")
    s_win = [ESem("s_win%d" % i) for i in range(4)]
    if paranoid:
        sc.psems = {"tensor": s_pe, "vector": s_vec, "scalar": s_act}

    def fence(*engs):
        tg = {"tensor": s_pe.n, "vector": s_vec.n, "scalar": s_act.n}
        for e in engs:
            for oe, sem_ in (("tensor", s_pe), ("vector", s_vec), ("scalar", s_act)):
                if oe != e and tg[oe] > 0:
                    sc.wait(e, sem_, tg[oe])

    sbp = lambda nm, shape, dt: es_p.enter_context(nc.sbuf_tensor(nm, shape, dt))
    qT = sbp("qT", [128, S], BF16)
    kT = sbp("kT", [128, S], BF16)
    id_sb = sbp("id_sb", [128, 128], BF16)
    onesb_sb = sbp("onesb_sb", [1, 128], BF16)
    onesf_sb = sbp("onesf_sb", [128, 1], F32)
    onesrf_sb = sbp("onesrf_sb", [1, 128], F32)
    bqk_sb = sbp("bqk_sb", [QKD, 1], F32)
    gq_sb = sbp("gq_sb", [QKD, 1], F32)
    bq_sb = sbp("bq_sb", [QKD, 1], F32)
    gk_sb = sbp("gk_sb", [QKD, 1], F32)
    bk_sb = sbp("bk_sb", [QKD, 1], F32)
    rinv1_all = sbp("rinv1_all", [128, T], F32)
    negmr_all = sbp("negmr_all", [128, T], F32)
    rs1_all = sbp("rs1_all", [128, T], F32)
    rs2_all = sbp("rs2_all", [128, T], F32)
    c_rne_t = sbp("c_rne_t", [128, 1], F32)
    invs2_t = sbp("invs2_t", [128, 1], F32)
    spacer_t = sbp("spacer_t", [128, 1024], F32)
    abscols = sbp("abscols", [128, 288], F32)
    thr_gu = sbp("thr_gu", [128, 1], F32)
    nthr_gu = sbp("nthr_gu", [128, 1], F32)
    ncm_gu = sbp("ncm_gu", [128, 1], F32)
    thr_dn = sbp("thr_dn", [128, 1], F32)
    nthr_dn = sbp("nthr_dn", [128, 1], F32)
    ncm_dn = sbp("ncm_dn", [128, 1], F32)
    red_tmp = sbp("red_tmp", [128, 2], F32)
    cm_row = sbp("cm_row", [1, 2], F32)
    cmb = sbp("cmb", [128, 2], F32)
    sg_col = sbp("sg_col", [128, 1], F32)
    sd_col = sbp("sd_col", [128, 1], F32)
    sg_all = sbp("sg_all", [128, T], F32)
    sd_all = sbp("sd_all", [128, T], F32)
    sf2_all = sbp("sf2_all", [128, T], F32)
    zero_bf = sbp("zero_bf", [128, 128], BF16)
    bnst_t = sbp("bnst_t", [128, 24], F32)
    dbg1_sb = sbp("dbg1_sb", [128, 8 * T], F32) if debug else None
    bnag_t = sbp("bnag_t", [128, 2], F32)
    pmx2 = sbp("pmx2", [128, 352], F32)
    stn = ("sum", "sq", "m", "msq", "m2", "varp", "vpe", "rinv2", "rinv",
           "negmr", "msv", "mse", "r2a", "rmsinv", "mx", "cmx", "icx", "sfac",
           "mx2", "cmx2", "icx2", "sfac2")
    st = {nm: sbp("st_" + nm, [128, 1], F32) for nm in stn}

    pf = [es_p.enter_context(nc.psum_tensor("pf%d" % i, [128, 512], F32)) for i in range(6)]
    pb = [es_p.enter_context(nc.psum_tensor("pb%d" % i, [128, 1024], BF16)) for i in range(2)]

    r_tp = Ring(2)

    def hbm_pkc(ap2d, a0, a1, c0, c1):
        return ap2d.rearrange("(a p) c -> p a c", p=128)[:, a0:a1, c0:c1]

    def sb_pkc(t, k, k0, k1, c0, c1):
        return t.rearrange("p (k c) -> p k c", k=k)[:, k0:k1, c0:c1]

    ld0 = DStream("c0", 1, "gpsimd")
    _, n0 = ld0.issue([
        (id_sb[:], ident[:, :]), (onesb_sb[:], ones_bf[:, :]),
        (onesf_sb[:], ones_f[:, :]), (onesrf_sb[:], ones_rf[:, :]),
        (bqk_sb[:], b_qk[:, :]), (gq_sb[:], g_q[:, :]), (bq_sb[:], be_q[:, :]),
        (gk_sb[:], g_k[:, :]), (bk_sb[:], be_k[:, :])])
    s_mem0 = ESem("s_mem0")
    if debug:
        sc.op("gpsimd", lambda h: h.memset(dbg1_sb[:], 0.0))
    sc.op("gpsimd", lambda h: h.memset(zero_bf[:], 0.0))
    sc.op("gpsimd", lambda h: h.memset(c_rne_t[:], C_RNE))
    sc.op("gpsimd", lambda h: h.memset(invs2_t[:], INV_S2))
    sc.op("gpsimd", lambda h: h.memset(spacer_t[:], 1.0), tick=s_mem0)

    es_r1 = ExitStack()
    xnT = es_r1.enter_context(nc.sbuf_tensor("xnT", [128, KD * S], BF16))

    es_a = ExitStack()
    sba = lambda nm, shape, dt: es_a.enter_context(nc.sbuf_tensor(nm, shape, dt))
    g_bcast = sba("g_bcast", [128, D], F32)
    nw_row = sba("nw_row", [1, D], F32)
    x_sb = [sba("x_sb%d" % i, [128, D], F32) for i in range(3)]
    bnag2 = [sba("bnag2_%d" % i, [128, 2], F32) for i in range(2)]
    rinv2_2 = [sba("rinv2_2_%d" % i, [128, 1], F32) for i in range(2)]
    rinv_2 = [sba("rinv_2_%d" % i, [128, 1], F32) for i in range(2)]
    rmsinv_2 = [sba("rmsinv_2_%d" % i, [128, 1], F32) for i in range(2)]
    sfac_2 = [sba("sfac_2_%d" % i, [128, 1], F32) for i in range(2)]
    xn_sb = [sba("xn_sb%d" % i, [128, D], BF16) for i in range(2)]
    xnorm_sb = [sba("xnorm%d" % i, [128, D], F32) for i in range(2)]
    y_sb = [sba("y_sb%d" % i, [128, D], F32) for i in range(2)]
    xq_sb = [sba("xq_sb%d" % i, [128, D], BF16) for i in range(2)]
    xqst_sb = [sba("xqst%d" % i, [128, D], BF16) for i in range(2)]
    trash1 = sba("trash1", [128, D], BF16)
    trash2 = sba("trash2", [128, D], BF16)

    ld_x = DStream("ldx", 3, "gpsimd")
    ld_nw = DStream("ldnw", 1, "gpsimd")
    r_xnorm = Ring(2)
    r_y = Ring(2)
    r_xn = Ring(2)
    us_xq = UStore("us_xq", 2)

    _, nnw = ld_nw.issue([(nw_row[:], nw[:, :])])
    sc.wait("tensor", ld_nw.load[0], nnw)
    sc.wait("tensor", ld0.load[0], n0)
    r_gb = Ring(2)
    for j in range(D // 512):
        slot = r_gb.acquire("tensor")
        ev = sc.op("tensor", lambda h, s=slot, j=j: h.matmul(
            pf[s][:], onesrf_sb[:], nw_row[:, 512 * j:512 * (j + 1)],
            start=True, stop=True), tick=s_pe)
        sc.wait_ev("vector", ev)
        ev_c = sc.op("vector", lambda h, s=slot, j=j: h.tensor_copy(
            g_bcast[:, 512 * j:512 * (j + 1)], pf[s][:]), tick=s_vec)
        r_gb.release(ev_c)
    v_gb = s_vec.n

    V = lambda fn, **kw: sc.op("vector", fn, **kw)
    SP = lambda **kw: sc.op("vector", lambda h: h.tensor_copy(
        spacer_t[:, 0:512], spacer_t[:, 512:1024]), **kw)
    p1 = {}

    def p1_S1a(t):
        s = t % 2
        xslot, lx = ld_x.issue([(x_sb[t % 3][:], x_in[128 * t:128 * (t + 1), :])])
        p1[("xs", t)] = xslot
        sc.wait("vector", ld_x.load[xslot], lx)
        if t == 0:
            sc.wait("vector", s_vec, v_gb)
        for j in range(4):
            V(lambda h, xs=xslot, j=j: h.bn_stats(
                bnst_t[:, 6 * j:6 * (j + 1)], x_sb[xs][:, 512 * j:512 * (j + 1)]))
        SP()
        V(lambda h, s=s: h.bn_aggr(bnag2[s][:], bnst_t[:]))
        SP()
        V(lambda h, s=s: h.tensor_scalar(st["vpe"][:], bnag2[s][:, 1:2], EPS, None, OP.add))
        SP()
        V(lambda h, s=s: h.reciprocal(rinv2_2[s][:], st["vpe"][:]))
        ev_r2 = SP(tick=s_vec)
        sc.wait_ev("scalar", ev_r2)
        p1[("ri", t)] = sc.op("scalar", lambda h, s=s: h.activation(
            rinv_2[s][:], rinv2_2[s][:], AF.Sqrt), tick=s_act)

    def p1_S1b(t):
        s = t % 2
        xslot = p1[("xs", t)]
        sc.wait_ev("vector", p1[("ri", t)])
        if debug:
            for di, src_ap in enumerate((bnag2[s][:, 1:2], rinv2_2[s][:], rinv_2[s][:])):
                V(lambda h, t=t, di=di, a=src_ap: h.tensor_copy(
                    dbg1_sb[:, 8 * t + di:8 * t + di + 1], a))
        V(lambda h, s=s, t=t: h.tensor_scalar(rinv1_all[:, t:t + 1], rinv_2[s][:], 1.0, None, OP.add))
        ev_nm = V(lambda h, s=s, t=t: h.tensor_scalar(
            negmr_all[:, t:t + 1], bnag2[s][:, 0:1], rinv_2[s][:], -1.0, OP.mult, OP.mult),
            tick=s_vec)
        V(lambda h, s=s: h.scalar_tensor_tensor(st["msv"][:], bnag2[s][:, 1:2], 1.0, rinv2_2[s][:], OP.mult, OP.mult))
        SP()
        V(lambda h: h.tensor_scalar(st["mse"][:], st["msv"][:], EPS, None, OP.add))
        SP()
        V(lambda h: h.reciprocal(st["r2a"][:], st["mse"][:]))
        ev_ra = SP(tick=s_vec)
        sc.wait_ev("scalar", ev_ra)
        p1[("rms", t)] = sc.op("scalar", lambda h, s=s: h.activation(
            rmsinv_2[s][:], st["r2a"][:], AF.Sqrt), tick=s_act)
        xnslot = r_xn.acquire("scalar")
        p1[("xn_s", t)] = xnslot
        sc.wait_ev("scalar", ev_nm)
        ev_xn = sc.op("scalar", lambda h, xs=xslot, ns=xnslot, s=s, t=t: h.activation(
            xn_sb[ns][:], x_sb[xs][:], AF.Identity, bias=negmr_all[:, t:t + 1], scale=rinv_2[s][:]),
            tick=s_act)
        p1[("xn", t)] = ev_xn
        ld_x.release(ev_xn)

    def p1_S2a(t):
        s = t % 2
        xnslot = p1[("xn_s", t)]
        sc.wait_ev("tensor", p1[("xn", t)])
        evs_tp = []
        for half in range(2):
            ps_ = r_tp.acquire("tensor")
            ev_tp = None
            for j in range(8):
                jj = 8 * half + j
                tk = s_pe if j == 7 else None
                e = sc.op("tensor", lambda h, ns=xnslot, ps=ps_, j=j, jj=jj: h.transpose(
                    pb[ps][:, 128 * j:128 * (j + 1)], xn_sb[ns][:, 128 * jj:128 * (jj + 1)],
                    id_sb[:]), tick=tk)
                ev_tp = e or ev_tp
            sc.wait_ev("scalar", ev_tp)
            dst = sb_pkc(xnT, KD, 8 * half, 8 * half + 8, 128 * t, 128 * (t + 1))
            src = pb[ps_].rearrange("p (k c) -> p k c", k=8)
            ev_ev = sc.op("scalar", lambda h, d=dst, s2=src: h.activation(
                d, s2, AF.Identity), tick=s_act)
            r_tp.release(ev_ev)
            evs_tp.append(ev_tp)

        nslot = r_xnorm.acquire("vector")
        sc.wait_ev("vector", p1[("rms", t)])
        V(lambda h, ns=xnslot, s2=nslot, s=s: h.scalar_tensor_tensor(
            xnorm_sb[s2][:], xn_sb[ns][:], rmsinv_2[s][:], g_bcast[:], OP.mult, OP.mult))
        V(lambda h, s2=nslot: h.tensor_reduce(
            st["mx"][:], xnorm_sb[s2][:], AX.X, OP.max, apply_absolute_value=True))
        SP()
        V(lambda h: h.tensor_scalar(st["cmx"][:], st["mx"][:], 1e-5, None, OP.max))
        SP()
        V(lambda h, t=t: h.tensor_scalar(rs1_all[:, t:t + 1], st["cmx"][:], 1.0 / 127, None, OP.mult))
        SP()
        V(lambda h: h.reciprocal(st["icx"][:], st["cmx"][:]))
        SP()
        V(lambda h, s=s: h.tensor_scalar(sfac_2[s][:], st["icx"][:], 127.0, None, OP.mult))
        ev_sf = SP(tick=s_vec)
        r_xn.release(ev_sf, evs_tp[1])
        yslot = r_y.acquire("scalar")
        p1[("y_s", t)] = yslot
        sc.wait_ev("scalar", ev_sf)
        if t == 0:
            sc.wait("scalar", s_mem0, 1)
        p1[("y", t)] = sc.op("scalar", lambda h, s2=nslot, ys=yslot, s=s: h.activation(
            y_sb[ys][:], xnorm_sb[s2][:], AF.Identity, bias=c_rne_t[:], scale=sfac_2[s][:]),
            tick=s_act)
        r_xnorm.release(p1[("y", t)])

    def p1_S2b(t):
        sq = t % 2
        yslot = p1[("y_s", t)]
        sc.wait_ev("vector", p1[("y", t)])
        ev_xq = V(lambda h, ys=yslot, sq=sq: h.tensor_scalar(
            xq_sb[sq][:], y_sb[ys][:], C_RNE, None, OP.subtract), tick=s_vec)
        r_y.release(ev_xq)

        stslot = us_xq.acquire("scalar")
        sc.wait_ev("tensor", ev_xq)
        for half in range(2):
            ps_ = r_tp.acquire("tensor")
            ev_tp = None
            for j in range(8):
                jj = 8 * half + j
                tk = s_pe if j == 7 else None
                e = sc.op("tensor", lambda h, sq=sq, ps=ps_, j=j, jj=jj: h.transpose(
                    pb[ps][:, 128 * j:128 * (j + 1)], xq_sb[sq][:, 128 * jj:128 * (jj + 1)],
                    id_sb[:]), tick=tk)
                ev_tp = e or ev_tp
            sc.wait_ev("scalar", ev_tp)
            ev_ev = sc.op("scalar", lambda h, ps=ps_, ss=stslot, hf=half: h.activation(
                xqst_sb[ss][:, 1024 * hf:1024 * (hf + 1)], pb[ps][:], AF.Identity), tick=s_act)
            r_tp.release(ev_ev)
        sc.wait("sync", s_act, s_act.n)
        dstq = hbm_pkc(xqT_d, 0, KD, 128 * t, 128 * (t + 1))
        srcq = sb_pkc(xqst_sb[stslot], KD, 0, KD, 0, 128)
        us_xq.issue(dstq, srcq)

    p1_S1a(0)
    p1_S1a(1)
    for t in range(T):
        p1_S1b(t)
        if t + 2 < T:
            p1_S1a(t + 2)
        if t > 0:
            p1_S2b(t - 1)
        p1_S2a(t)
    p1_S2b(T - 1)
    v_p1 = s_vec.n

    es_a.close()

    es_b = ExitStack()
    sbb = lambda nm, shape, dt: es_b.enter_context(nc.sbuf_tensor(nm, shape, dt))
    whb = [sbb("whb%d" % i, [128, KD * 512], BF16) for i in range(2)]
    hstg = sbb("hstg0", [128, T * 512], BF16)
    qk_bf = sbb("qk_bf", [128, 512], BF16)
    wqk_b = sbb("wqk_b", [128, KD * 128], BF16)
    bhid_row = sbb("bhid_row", [1, HID2], BF16)

    fence("gpsimd", "sync", "scalar")
    us_xq.wait_all("gpsimd")
    us_xq.wait_all("scalar")

    ld_bh = DStream("ldbh", 1, "gpsimd")
    ld_qk = DStream("ldqk", 1, "gpsimd")
    ld_wh = DStream("ldwh", 2, "gpsimd")
    us_hid = UStore("us_hid", 1)
    r_hps = Ring(4)
    r_qkps = Ring(2)

    _, nbh = ld_bh.issue([(bhid_row[:], b_hid[:, :])])
    sc.wait("tensor", ld_bh.load[0], nbh)

    _, nqk = ld_qk.issue([(wqk_b[:, 128 * k:128 * (k + 1)],
                           w_qk[128 * k:128 * (k + 1), :]) for k in range(KD)])
    sc.wait("tensor", ld_qk.load[0], nqk)
    sc.wait("tensor", s_vec, v_p1)
    for strip in range(4):
        pslot = r_qkps.acquire("tensor")
        ev_mm = None
        for k in range(KD):
            tk = s_pe if k == KD - 1 else None
            e = sc.op("tensor", lambda h, ps=pslot, k=k, sp=strip: h.matmul(
                pf[4 + ps][:], wqk_b[:, 128 * k:128 * (k + 1)],
                xnT[:, S * k + 512 * sp: S * k + 512 * (sp + 1)],
                start=(k == 0), stop=(k == KD - 1)), tick=tk)
            ev_mm = e or ev_mm
        sc.wait_ev("scalar", ev_mm)
        ev_si = sc.op("scalar", lambda h, ps=pslot: h.activation(
            qk_bf[:], pf[4 + ps][:], AF.Silu, bias=bqk_sb[:]), tick=s_act)
        r_qkps.release(ev_si)
        sc.wait_ev("vector", ev_si)
        sc.op("vector", lambda h, sp=strip: h.tensor_scalar(
            qT[:, 512 * sp:512 * (sp + 1)], qk_bf[:], gq_sb[:], bq_sb[:], OP.mult, OP.add))
        sc.op("vector", lambda h, sp=strip: h.tensor_scalar(
            kT[:, 512 * sp:512 * (sp + 1)], qk_bf[:], gk_sb[:], bk_sb[:], OP.mult, OP.add),
            tick=s_vec)
    v_qk = s_vec.n

    mean_jobs = []
    for c in range(OGU // 128):
        for hh in range(2):
            mean_jobs.append((w_gu[128 * c:128 * (c + 1), 1024 * hh:1024 * (hh + 1)], 1024))
    NJ_GU = len(mean_jobs)
    for rb in range(KD):
        for c0 in range(0, FF, 1024):
            w = min(1024, FF - c0)
            mean_jobs.append((w_dn[128 * rb:128 * (rb + 1), c0:c0 + w], w))
    NMJ = len(mean_jobs)
    m_iss = [0]
    m_con = [0]

    def mean_issue_one():
        i = m_iss[0]
        src, w = mean_jobs[i]
        sc.op("scalar", lambda h, s=i % 3, src=src, w=w: h.dma_start(
            wmean_b[s][:, :w], src), tick=s_win[i % 3], inc=16)
        m_iss[0] += 1

    def mean_pump(k):
        for _ in range(k):
            if m_con[0] >= NMJ:
                return
            while m_iss[0] < NMJ and m_iss[0] < m_con[0] + 3:
                mean_issue_one()
            i = m_con[0]
            w = mean_jobs[i][1]
            sc.wait("scalar", s_win[i % 3], 16 * (i // 3 + 1))
            sc.op("scalar", lambda h, s=i % 3, w=w, i=i: h.activation(
                wmean_b[s][:, :w], wmean_b[s][:, :w], AF.Abs, accum_out=abscols[:, i:i + 1]),
                tick=s_act)
            m_con[0] += 1

    for strip in range(HID2 // 512):
        wnext = ld_wh.issued % 2
        wslot, lw = ld_wh.issue([(whb[wnext][:, 512 * k:512 * (k + 1)],
                                  w_hid[128 * k:128 * (k + 1), 512 * strip:512 * (strip + 1)])
                                 for k in range(KD)])
        us_hid.acquire("scalar")
        sc.wait("tensor", ld_wh.load[wslot], lw)
        for t in range(T):
            pslot = r_hps.acquire("tensor")
            sc.op("tensor", lambda h, ps=pslot, sp=strip: h.matmul(
                pf[ps][:], onesb_sb[:], bhid_row[:, 512 * sp:512 * (sp + 1)],
                start=True, stop=False))
            ev_mm = None
            for k in range(KD):
                tk = s_pe if k == KD - 1 else None
                e = sc.op("tensor", lambda h, ps=pslot, k=k, t=t, ws=wslot: h.matmul(
                    pf[ps][:], xnT[:, S * k + 128 * t: S * k + 128 * (t + 1)],
                    whb[ws][:, 512 * k:512 * (k + 1)],
                    start=False, stop=(k == KD - 1)), tick=tk)
                ev_mm = e or ev_mm
            if t == T - 1:
                ld_wh.release(ev_mm)
            sc.wait_ev("scalar", ev_mm)
            ev_si = sc.op("scalar", lambda h, ps=pslot, t=t: h.activation(
                hstg[:, 512 * t:512 * (t + 1)], pf[ps][:], AF.Silu), tick=s_act)
            r_hps.release(ev_si)
        sc.wait("sync", s_act, s_act.n)
        dsth = hbm_pkc(hid_d, 0, T, 512 * strip, 512 * (strip + 1))
        srch = sb_pkc(hstg, T, 0, T, 0, 512)
        us_hid.issue(dsth, srch)
    a_p2 = s_act.n

    es_b.close()
    es_r1.close()

    es_r2 = ExitStack()
    acc = es_r2.enter_context(nc.sbuf_tensor("acc", [128, T * D], BF16))

    es_c = ExitStack()
    sbc = lambda nm, shape, dt: es_c.enter_context(nc.sbuf_tensor(nm, shape, dt))
    bout_row = sbc("bout_row", [1, D], BF16)
    vq_sb = [sbc("vq_sb%d" % i, [128, T * 1024], BF16) for i in range(2)]
    gq_t = [sbc("gq_t%d" % i, [128, 1024], BF16) for i in range(2)]
    woq_sb = sbc("woq_sb", [128, 8 * D], BF16)
    wmean_b = [sbc("wmean%d" % i, [128, 1024], F32) for i in range(3)]
    a_stage = sbc("a_stage", [128, 1024], BF16)
    a_tmp = [sbc("a_tmp%d" % i, [128, 512], BF16) for i in range(2)]
    avg_stage = sbc("avg_stage", [128, 1024], BF16)
    avgT_stage = sbc("avgT_stage", [128, 1024], BF16)

    fence("gpsimd", "sync", "vector")
    us_hid.wait_all("vector")

    ld_bo = DStream("ldbo", 1, "gpsimd")
    ld_v = DStream("ldv", 2, "gpsimd")
    ld_g = DStream("ldg", 2, "gpsimd")
    ld_wo = DStream("ldwo", 2, "gpsimd")
    r_sim = Ring(2)
    r_av = Ring(1)
    r_out = Ring(1)
    r_ast = Ring(2)
    r_atmp = Ring(2)
    r_avgs = Ring(1)
    r_avT = Ring(1)

    _, nbo = ld_bo.issue([(bout_row[:], b_out[:, :])])
    sc.wait("tensor", ld_bo.load[0], nbo)
    sc.wait("tensor", s_act, a_p2)

    us_hid.wait_all("gpsimd")
    ev_wo_last = None
    vslot0, lv0 = ld_v.issue([(vq_sb[0][:, 1024 * t:1024 * (t + 1)],
                               hid_d[128 * t:128 * (t + 1), 0:1024]) for t in range(T)])
    vload = {0: (vslot0, lv0)}
    for hp in range(4):
        vslot, lv = vload[hp]
        lwo_h = []
        for dh in range(2):
            _, lw_ = ld_wo.issue([(woq_sb[:, D * k + 1024 * dh:D * k + 1024 * (dh + 1)],
                                   w_out[1024 * hp + 128 * k:1024 * hp + 128 * (k + 1),
                                         1024 * dh:1024 * (dh + 1)])
                                  for k in range(8)])
            lwo_h.append(lw_)
        if hp + 1 < 4:
            vload[hp + 1] = ld_v.issue(
                [(vq_sb[(hp + 1) % 2][:, 1024 * t:1024 * (t + 1)],
                  hid_d[128 * t:128 * (t + 1), 1024 * (hp + 1):1024 * (hp + 2)])
                 for t in range(T)])
        if hp == 0:
            sc.wait("tensor", s_vec, v_qk)
        sim_evs = {}
        sim_slots = {}

        def emit_sims(t, g, hp=hp, vslot=vslot, lv=lv):
            pslot = r_sim.acquire("tensor")
            if t == 0 and g == 0:
                sc.wait("tensor", ld_v.load[vslot], lv)
            ev_sim = None
            for j in range(4):
                kt = 4 * g + j
                tk = s_pe if j == 3 else None
                e = sc.op("tensor", lambda h, ps=pslot, kt=kt, j=j, t=t: h.matmul(
                    pf[ps][:, 128 * j:128 * (j + 1)],
                    kT[:, 128 * kt:128 * (kt + 1)],
                    qT[:, 128 * t:128 * (t + 1)],
                    start=True, stop=True), tick=tk)
                ev_sim = e or ev_sim
            sim_evs[(t, g)] = ev_sim
            sim_slots[(t, g)] = pslot

        emit_sims(0, 0)
        emit_sims(0, 1)
        for t in range(T):
            gslot, lg = ld_g.issue([(gq_t[ld_g.issued % 2][:],
                                     hid_d[128 * t:128 * (t + 1),
                                           HV + 1024 * hp:HV + 1024 * (hp + 1)])])
            r_av.acquire("tensor")
            ev_av = None
            for g in range(4):
                pslot = sim_slots.pop((t, g))
                aslot = r_ast.acquire("vector")
                atslot = r_atmp.acquire("scalar")
                sc.wait_ev("scalar", sim_evs.pop((t, g)))
                ev_relu = sc.op("scalar", lambda h, ps=pslot, ats=atslot: h.activation(
                    a_tmp[ats][:], pf[ps][:], AF.Relu), tick=s_act)
                r_sim.release(ev_relu)
                sc.wait_ev("vector", ev_relu)
                ev_a = sc.op("vector", lambda h, ats=atslot, a=aslot: h.tensor_tensor(
                    a_stage[:, 512 * a:512 * (a + 1)], a_tmp[ats][:], a_tmp[ats][:],
                    OP.mult), tick=s_vec)
                r_atmp.release(ev_a)
                if g + 2 < 4:
                    emit_sims(t, g + 2)
                elif t + 1 < T:
                    emit_sims(t + 1, g - 2)
                sc.wait_ev("tensor", ev_a)
                for j in range(4):
                    kt = 4 * g + j
                    for hf in range(2):
                        tk = s_pe if (j == 3 and hf == 1) else None
                        e = sc.op("tensor", lambda h, j=j, kt=kt, hf=hf, a=aslot, vs=vslot: h.matmul(
                            pf[2 + hf][:],
                            a_stage[:, 512 * a + 128 * j:512 * a + 128 * (j + 1)],
                            vq_sb[vs][:, 1024 * kt + 512 * hf:1024 * kt + 512 * (hf + 1)],
                            start=(kt == 0), stop=(kt == T - 1)), tick=tk)
                        ev_av = e or ev_av
                r_ast.release(ev_av)
            mean_pump(5)
            r_avgs.acquire("vector")
            sc.wait_ev("vector", ev_av)
            sc.wait("vector", ld_g.load[gslot], lg)
            if hp == 0 and t == 0:
                sc.wait("vector", s_mem0, 1)
            r_avT.acquire("tensor")
            ev_aT_h = []
            ev_avg = None
            for hf in range(2):
                e_avg = sc.op("vector", lambda h, hf=hf, gs=gslot: h.scalar_tensor_tensor(
                    avg_stage[:, 512 * hf:512 * (hf + 1)], pf[2 + hf][:], invs2_t[:],
                    gq_t[gs][:, 512 * hf:512 * (hf + 1)],
                    OP.mult, OP.mult), tick=s_vec)
                ev_avg = e_avg
                ps_h = r_tp.acquire("tensor")
                sc.wait_ev("tensor", e_avg)
                ev_tp = None
                for j in range(4):
                    jj = 4 * hf + j
                    tk = s_pe if j == 3 else None
                    e = sc.op("tensor", lambda h, ps=ps_h, j=j, jj=jj: h.transpose(
                        pb[ps][:, 128 * j:128 * (j + 1)], avg_stage[:, 128 * jj:128 * (jj + 1)],
                        id_sb[:]), tick=tk)
                    ev_tp = e or ev_tp
                if hf == 1:
                    r_avgs.release(ev_tp)
                sc.wait_ev("vector", ev_tp)
                e_aT = sc.op("vector", lambda h, ps=ps_h, hf=hf: h.tensor_copy(
                    avgT_stage[:, 512 * hf:512 * (hf + 1)], pb[ps][:, 0:512]), tick=s_vec)
                ev_aT_h.append(e_aT)
                r_tp.release(e_aT)
            r_av.release(ev_avg)
            ld_g.release(ev_avg)
            for dh in range(2):
                sc.wait("tensor", ld_wo.load[dh], lwo_h[dh])
                r_out.acquire("tensor")
                ev_wo = None
                if hp == 0:
                    for j2 in range(2):
                        sc.op("tensor", lambda h, dh=dh, j2=j2: h.matmul(
                            pf[4 + j2][:], onesb_sb[:],
                            bout_row[:, 1024 * dh + 512 * j2:1024 * dh + 512 * (j2 + 1)],
                            start=True, stop=False))
                for hc in range(8):
                    if dh == 0 and hc == 0:
                        sc.wait_ev("tensor", ev_aT_h[0])
                    elif dh == 0 and hc == 4:
                        sc.wait_ev("tensor", ev_aT_h[1])
                    for j2 in range(2):
                        tk = s_pe if (hc == 7 and j2 == 1) else None
                        e = sc.op("tensor", lambda h, dh=dh, hc=hc, j2=j2, hp=hp: h.matmul(
                            pf[4 + j2][:],
                            avgT_stage[:, 128 * hc:128 * (hc + 1)],
                            woq_sb[:, D * hc + 1024 * dh + 512 * j2:
                                   D * hc + 1024 * dh + 512 * (j2 + 1)],
                            start=(hc == 0 and hp != 0), stop=(hc == 7)), tick=tk)
                        ev_wo = e or ev_wo
                if dh == 1:
                    r_avT.release(ev_wo)
                if t == T - 1:
                    ld_wo.release(ev_wo)
                    if dh == 1 and hp == 3:
                        ev_wo_last = ev_wo
                sc.wait_ev("vector", ev_wo)
                ev_ac = None
                for j2 in range(2):
                    dsta = acc[:, D * t + 1024 * dh + 512 * j2: D * t + 1024 * dh + 512 * (j2 + 1)]
                    tk = s_vec if j2 == 1 else None
                    if hp == 0:
                        e = sc.op("vector", lambda h, d=dsta, j2=j2: h.tensor_copy(
                            d, pf[4 + j2][:]), tick=tk)
                    else:
                        e = sc.op("vector", lambda h, d=dsta, j2=j2: h.tensor_tensor(
                            d, pf[4 + j2][:], d, OP.add), tick=tk)
                    ev_ac = e or ev_ac
                r_out.release(ev_ac)
            if t == T - 1:
                ld_v.release(ev_av)
    v_p3 = s_vec.n

    es_c.close()

    mean_pump(NMJ)
    a_means = s_act.n
    sc.wait("vector", s_act, a_means)
    V2 = lambda fn, **kw: sc.op("vector", fn, **kw)
    SP2 = lambda **kw: sc.op("vector", lambda h: h.tensor_copy(
        spacer_t[:, 0:512], spacer_t[:, 512:1024]), **kw)
    V2(lambda h: h.tensor_reduce(red_tmp[:, 0:1], abscols[:, 0:NJ_GU], AX.X, OP.add))
    V2(lambda h, a=NJ_GU, b=len(mean_jobs): h.tensor_reduce(
        red_tmp[:, 1:2], abscols[:, a:b], AX.X, OP.add))
    ev_red = SP2(tick=s_vec)
    sc.wait_ev("tensor", ev_red)
    sc.wait_ev("tensor", ev_wo_last)
    sc.wait("tensor", s_vec, v_p3)
    sc.op("tensor", lambda h: h.matmul(pf[0][0:1, 0:1], red_tmp[:, 0:1], onesf_sb[:],
                                       start=True, stop=True))
    ev_ps = sc.op("tensor", lambda h: h.matmul(pf[0][0:1, 1:2], red_tmp[:, 1:2], onesf_sb[:],
                                               start=True, stop=True), tick=s_pe)
    sc.wait_ev("vector", ev_ps)
    V2(lambda h: h.tensor_copy(cm_row[:, :], pf[0][0:1, 0:2]))
    SP2()
    V2(lambda h: h.tensor_scalar(cm_row[:, 0:1], cm_row[:, 0:1], 1.0 / (OGU * D), None, OP.mult))
    V2(lambda h: h.tensor_scalar(cm_row[:, 1:2], cm_row[:, 1:2], 1.0 / (D * FF), None, OP.mult))
    SP2()
    V2(lambda h: h.tensor_scalar(cm_row[:, :], cm_row[:, :], 1e-5, None, OP.max))
    ev_cm = SP2(tick=s_vec)
    sc.wait_ev("tensor", ev_cm)
    ev_bc = sc.op("tensor", lambda h: h.matmul(pf[1][:, 0:2], onesrf_sb[:], cm_row[:, :],
                                               start=True, stop=True), tick=s_pe)
    sc.wait_ev("vector", ev_bc)
    V2(lambda h: h.tensor_copy(cmb[:, :], pf[1][:, 0:2]))
    SP2()
    V2(lambda h: h.tensor_scalar(thr_gu[:], cmb[:, 0:1], 0.5, None, OP.mult))
    V2(lambda h: h.tensor_scalar(nthr_gu[:], cmb[:, 0:1], -0.5, None, OP.mult))
    V2(lambda h: h.tensor_scalar(ncm_gu[:], cmb[:, 0:1], -1.0, None, OP.mult))
    V2(lambda h: h.tensor_scalar(thr_dn[:], cmb[:, 1:2], 0.5, None, OP.mult))
    V2(lambda h: h.tensor_scalar(nthr_dn[:], cmb[:, 1:2], -0.5, None, OP.mult))
    ev_thr = V2(lambda h: h.tensor_scalar(ncm_dn[:], cmb[:, 1:2], -1.0, None, OP.mult),
                tick=s_vec)

    es_d = ExitStack()
    sbd = lambda nm, shape, dt: es_d.enter_context(nc.sbuf_tensor(nm, shape, dt))
    xqT_sb = sbd("xqT_sb", [128, KD * S], BF16)
    wqT_g = [sbd("wqT_g%d" % i, [128, KD * 256], BF16) for i in range(2)]
    wqT_u = [sbd("wqT_u%d" % i, [128, KD * 256], BF16) for i in range(2)]
    wg_in = [sbd("wg_in%d" % i, [128, D], F32) for i in range(1)]
    c1_bf = sbd("c1_bf", [128, D], BF16)
    wqn_bf = [sbd("wqn%d" % i, [128, D], BF16) for i in range(2)]
    g_blk = sbd("g_blk", [128, 256], BF16)
    gup_blk = sbd("gup_blk", [128, 256], BF16)
    h2stg = sbd("h2stg", [128, T * 256], BF16)

    fence("gpsimd", "sync")

    ld_xqT = DStream("ldxqT", 1, "gpsimd")
    ld_wg = DStream("ldwg", 1, "gpsimd")
    us_h2 = UStore("us_h2", 1)
    r_wqn = Ring(2)
    r_gups = Ring(2)
    r_gblk = Ring(1)

    sc.wait_ev("vector", ev_thr)
    sc.op("vector", lambda h: h.tensor_copy(spacer_t[:, 0:512], spacer_t[:, 512:1024]))
    ev_sg = sc.op("vector", lambda h: h.tensor_scalar(
        sg_all[:], rs1_all[:], ncm_gu[:], None, OP.mult), tick=s_vec)
    us_xq.wait_all("gpsimd")
    _, lxq = ld_xqT.issue([(sb_pkc(xqT_sb, KD, 0, KD, 0, S),
                            hbm_pkc(xqT_d, 0, KD, 0, S))])
    sc.wait("tensor", ld_xqT.load[0], lxq)
    sc.wait_ev("tensor", ev_thr)

    NP = OGU // 512
    r_wqT = Ring(2)
    build_done = {}
    pend_q = {}

    def emit_build_quant(jp, which, c, first=False):
        base = 256 * jp + (FF if which else 0)
        wslot, lwg = ld_wg.issue([(wg_in[0][:],
                                   w_gu[base + 128 * c: base + 128 * (c + 1), :])])
        qslot = r_wqn.acquire("vector")
        sc.wait("vector", ld_wg.load[wslot], lwg)
        if first:
            sc.wait_ev("vector", ev_thr)
        sc.op("vector", lambda h, ws=wslot: h.tensor_scalar(
            c1_bf[:], wg_in[ws][:], thr_gu[:], None, OP.is_gt))
        ev_q = sc.op("vector", lambda h, ws=wslot, qs=qslot: h.scalar_tensor_tensor(
            wqn_bf[qs][:], wg_in[ws][:], nthr_gu[:], c1_bf[:], OP.is_lt, OP.subtract),
            tick=s_vec)
        ld_wg.release(ev_q)
        pend_q[(jp, which, c)] = (qslot, ev_q)

    def emit_build_tp(jp, which, c, buf):
        qslot, ev_q = pend_q.pop((jp, which, c))
        sc.wait_ev("tensor", ev_q)
        wqT_t = (wqT_g if which == 0 else wqT_u)[buf]
        for half in range(2):
            ps_ = r_tp.acquire("tensor")
            ev_tp = None
            for jj in range(8):
                kk = 8 * half + jj
                tk = s_pe if jj == 7 else None
                e = sc.op("tensor", lambda h, qs=qslot, ps=ps_, jj=jj, kk=kk: h.transpose(
                    pb[ps][:, 128 * jj:128 * (jj + 1)],
                    wqn_bf[qs][:, 128 * kk:128 * (kk + 1)], id_sb[:]), tick=tk)
                ev_tp = e or ev_tp
            if half == 1:
                r_wqn.release(ev_tp)
            sc.wait_ev("vector", ev_tp)
            dstw = sb_pkc(wqT_t, KD, 8 * half, 8 * half + 8, 128 * c, 128 * (c + 1))
            srcw = pb[ps_].rearrange("p (k c2) -> p k c2", k=8)
            ev_ev = sc.op("vector", lambda h, d=dstw, s2=srcw: h.tensor_copy(d, s2),
                          tick=s_vec)
            r_tp.release(ev_ev)

    def emit_build_chunk(jp, which, c, buf, first=False):
        emit_build_quant(jp, which, c, first=first)
        emit_build_tp(jp, which, c, buf)

    r_wqT.acquire("vector")
    for which in (0, 1):
        for c in (0, 1):
            emit_build_chunk(0, which, c, 0, first=(which == 0 and c == 0))
    build_done[0] = s_vec.n

    for jp in range(NP):
        buf = jp % 2
        sc.wait("tensor", s_vec, build_done[jp])
        r_gblk.acquire("vector")
        us_h2.acquire("vector")
        chunks = [(w, c) for w in (0, 1) for c in (0, 1)] if jp + 1 < NP else []
        qchunks = list(chunks)
        if chunks:
            r_wqT.acquire("vector")
        ev_h2 = None
        ev_gu_last = None
        for t in range(T):
            pp = r_gups.acquire("tensor")
            for k in range(KD):
                sc.op("tensor", lambda h, pp=pp, k=k, t=t, b=buf: h.matmul(
                    pf[2 * pp][:, 0:256], xqT_sb[:, S * k + 128 * t: S * k + 128 * (t + 1)],
                    wqT_g[b][:, 256 * k:256 * (k + 1)],
                    start=(k == 0), stop=(k == KD - 1)))
            ev_gu = None
            for k in range(KD):
                tk = s_pe if k == KD - 1 else None
                e = sc.op("tensor", lambda h, pp=pp, k=k, t=t, b=buf: h.matmul(
                    pf[2 * pp + 1][:, 0:256], xqT_sb[:, S * k + 128 * t: S * k + 128 * (t + 1)],
                    wqT_u[b][:, 256 * k:256 * (k + 1)],
                    start=(k == 0), stop=(k == KD - 1)), tick=tk)
                ev_gu = e or ev_gu
            ev_gu_last = ev_gu
            sc.wait_ev("vector", ev_gu)
            V3 = lambda fn, **kw: sc.op("vector", fn, **kw)
            V3(lambda h, pp=pp, t=t: h.tensor_scalar(
                g_blk[:], pf[2 * pp][:, 0:256], sg_all[:, t:t + 1], None, OP.mult))
            ev_gup = V3(lambda h, pp=pp, t=t: h.scalar_tensor_tensor(
                gup_blk[:], pf[2 * pp + 1][:, 0:256], sg_all[:, t:t + 1], g_blk[:],
                OP.mult, OP.mult), tick=s_vec)
            r_gups.release(ev_gup)
            ev_h2 = V3(lambda h, t=t: h.scalar_tensor_tensor(
                h2stg[:, 256 * t:256 * (t + 1)], g_blk[:], 0.0, gup_blk[:], OP.max, OP.mult),
                tick=s_vec)
            if t > 0:
                V3(lambda h, tp_=t - 1, jp=jp: h.tensor_reduce(
                    pmx2[:, 22 * tp_ + jp:22 * tp_ + jp + 1],
                    h2stg[:, 256 * tp_:256 * (tp_ + 1)], AX.X, OP.max,
                    apply_absolute_value=True))
            if qchunks and t in (1, 4, 7, 10):
                emit_build_quant(jp + 1, *qchunks.pop(0))
            if chunks and t in (3, 6, 9, 12):
                emit_build_tp(jp + 1, *chunks.pop(0), 1 - buf)
        build_done[jp + 1] = s_vec.n
        r_wqT.release(ev_gu_last)
        r_gblk.release(ev_h2)
        sc.op("vector", lambda h: h.tensor_copy(
            spacer_t[:, 0:512], spacer_t[:, 512:1024]))
        sc.op("vector", lambda h, jp=jp: h.tensor_reduce(
            pmx2[:, 22 * (T - 1) + jp:22 * (T - 1) + jp + 1],
            h2stg[:, 256 * (T - 1):256 * T], AX.X, OP.max,
            apply_absolute_value=True))
        sc.wait("sync", s_vec, s_vec.n)
        dsth2 = hbm_pkc(h2_d, 0, T, 256 * jp, 256 * (jp + 1))
        srch2 = sb_pkc(h2stg, T, 0, T, 0, 256)
        us_h2.issue(dsth2, srch2)

    es_d.close()

    es_d2 = ExitStack()
    sbd2 = lambda nm, shape, dt: es_d2.enter_context(nc.sbuf_tensor(nm, shape, dt))
    h2row = [sbd2("h2row%d" % i, [128, FF], BF16) for i in range(2)]
    y2_f = sbd2("y2_f", [128, FF], F32)
    hq_bf = sbd2("hq_bf", [128, FF], BF16)
    hqst = [sbd2("hqst%d" % i, [128, KF * 128], BF16) for i in range(2)]

    fence("gpsimd", "sync", "vector", "scalar")
    us_h2.wait_all("vector")
    us_h2.wait_all("scalar")

    ld_h2 = DStream("ldh2", 2, "gpsimd")
    us_hq = UStore("us_hq", 2)
    r_y2 = Ring(1)
    r_hq = Ring(1)

    us_h2.wait_all("gpsimd")
    sfac2_all = st
    V4 = lambda fn, **kw: sc.op("vector", fn, **kw)
    SP4 = lambda **kw: sc.op("vector", lambda h: h.tensor_copy(
        spacer_t[:, 0:512], spacer_t[:, 512:1024]), **kw)
    evA = {}
    hslots = {}

    def emit_A_load(t):
        hnext = ld_h2.issued % 2
        hslot, lh2 = ld_h2.issue([(h2row[hnext][:], h2_d[128 * t:128 * (t + 1), :])])
        hslots[t] = (hslot, lh2)

    def emit_A_chain(t):
        hslot, lh2 = hslots[t]
        sc.wait("vector", ld_h2.load[hslot], lh2)
        V4(lambda h, t=t: h.tensor_reduce(
            st["mx2"][:], pmx2[:, 22 * t:22 * (t + 1)], AX.X, OP.max,
            apply_absolute_value=True))
        SP4()
        V4(lambda h: h.tensor_scalar(st["cmx2"][:], st["mx2"][:], 1e-5, None, OP.max))
        SP4()
        V4(lambda h, t=t: h.tensor_scalar(rs2_all[:, t:t + 1], st["cmx2"][:], 1.0 / 127, None, OP.mult))
        SP4()
        V4(lambda h: h.reciprocal(st["icx2"][:], st["cmx2"][:]))
        SP4()
        V4(lambda h, t=t: h.tensor_scalar(sf2_all[:, t:t + 1], st["icx2"][:], 127.0, None, OP.mult))
        evA[t] = SP4(tick=s_vec)

    emit_A_load(0)
    emit_A_chain(0)
    for t in range(T):
        if t + 1 < T:
            emit_A_load(t + 1)
        hslot = hslots[t][0]
        r_y2.acquire("scalar")
        sc.wait_ev("scalar", evA.pop(t))
        ev_y2_h = []
        for hh in range(2):
            e = sc.op("scalar", lambda h, s=hslot, t=t, hh=hh: h.activation(
                y2_f[:, 2816 * hh:2816 * (hh + 1)], h2row[s][:, 2816 * hh:2816 * (hh + 1)],
                AF.Identity, bias=c_rne_t[:], scale=sf2_all[:, t:t + 1]),
                tick=s_act)
            ev_y2_h.append(e)
        ld_h2.release(ev_y2_h[1])
        r_hq.acquire("vector")
        qslot = us_hq.acquire("scalar")
        ev_hq = None
        for hh in range(2):
            sc.wait_ev("vector", ev_y2_h[hh])
            ev_hq = V4(lambda h, hh=hh: h.tensor_scalar(
                hq_bf[:, 2816 * hh:2816 * (hh + 1)], y2_f[:, 2816 * hh:2816 * (hh + 1)],
                C_RNE, None, OP.subtract), tick=s_vec)
            base = 22 * hh
            done = 0
            sc.wait_ev("tensor", ev_hq)
            while done < 22:
                cnt = min(8, 22 - done)
                ps_ = r_tp.acquire("tensor")
                ev_tp = None
                for jj in range(cnt):
                    kk = base + done + jj
                    tk = s_pe if jj == cnt - 1 else None
                    e = sc.op("tensor", lambda h, ps=ps_, jj=jj, kk=kk: h.transpose(
                        pb[ps][:, 128 * jj:128 * (jj + 1)], hq_bf[:, 128 * kk:128 * (kk + 1)],
                        id_sb[:]), tick=tk)
                    ev_tp = e or ev_tp
                if hh == 1 and done + cnt >= 22:
                    r_hq.release(ev_tp)
                sc.wait_ev("scalar", ev_tp)
                dsts = hqst[qslot][:, 128 * (base + done):128 * (base + done + cnt)]
                ev_ev = sc.op("scalar", lambda h, ps=ps_, d=dsts, c=cnt: h.activation(
                    d, pb[ps][:, 0:128 * c], AF.Identity), tick=s_act)
                r_tp.release(ev_ev)
                done += cnt
        r_y2.release(ev_hq)
        if t + 1 < T:
            emit_A_chain(t + 1)
        sc.wait("sync", s_act, s_act.n)
        dsthq = hqT_d[t].rearrange("(k p) c -> p k c", p=128)
        srchq = sb_pkc(hqst[qslot], KF, 0, KF, 0, 128)
        us_hq.issue(dsthq, srchq)

    es_d2.close()

    es_e = ExitStack()
    sbe = lambda nm, shape, dt: es_e.enter_context(nc.sbuf_tensor(nm, shape, dt))
    wdT_q = sbe("wdT_q", [128, KF * 512], BF16)
    wd_in = [sbe("wd_in%d" % i, [128, 2816], F32) for i in range(3)]
    wqnd_bf = [sbe("wqnd%d" % i, [128, 2816], BF16) for i in range(2)]
    hq_in = [sbe("hq_in%d" % i, [128, KF * 128], BF16) for i in range(2)]
    x5_in = [sbe("x5_in%d" % i, [128, 512], F32) for i in range(2)]
    outst = [sbe("outst%d" % i, [128, 512], F32) for i in range(2)]
    ffnst = [sbe("ffnst%d" % i, [128, 512], F32) for i in range(2)]

    fence("gpsimd", "sync", "vector", "scalar")
    us_hq.wait_all("gpsimd")
    us_hq.wait_all("vector")
    us_hq.wait_all("scalar")

    ld_wd = DStream("ldwd", 3, "gpsimd")
    ld_hq = DStream("ldhq", 2, "sync")
    ld_x5 = DStream("ldx5", 2, "gpsimd")
    us_out = UStore("us_out", 2, qeng="scalar")
    r_wqnd = Ring(2)
    r_dnps = Ring(2)
    r_wdT = Ring(1)

    us_hq.wait_all("sync")
    sc.op("vector", lambda h: h.tensor_copy(spacer_t[:, 0:512], spacer_t[:, 512:1024]))
    sc.op("vector", lambda h: h.tensor_scalar(
        sd_all[:], rs2_all[:], ncm_dn[:], None, OP.mult))
    for q in range(4):
        r_wdT.acquire("vector")
        for c in range(4):
            for hh in range(2):
                wnext = ld_wd.issued % 3
                wslot, lwd = ld_wd.issue([(wd_in[wnext][:],
                                           w_dn[512 * q + 128 * c:512 * q + 128 * (c + 1),
                                                2816 * hh:2816 * (hh + 1)])])
                qslot = r_wqnd.acquire("vector")
                sc.wait("vector", ld_wd.load[wslot], lwd)
                sc.op("vector", lambda h, ws=wslot, qs=qslot: h.tensor_scalar(
                    wqnd_bf[qs][:], wd_in[ws][:], thr_dn[:], None, OP.is_gt))
                ev_qd = sc.op("vector", lambda h, ws=wslot, qs=qslot: h.scalar_tensor_tensor(
                    wqnd_bf[qs][:], wd_in[ws][:], nthr_dn[:], wqnd_bf[qs][:], OP.is_lt, OP.subtract),
                    tick=s_vec)
                ld_wd.release(ev_qd)
                sc.wait_ev("tensor", ev_qd)
                done = 0
                while done < 22:
                    cnt = min(8, 22 - done)
                    ps_ = r_tp.acquire("tensor")
                    ev_tp = None
                    for jj in range(cnt):
                        kk = done + jj
                        tk = s_pe if jj == cnt - 1 else None
                        e = sc.op("tensor", lambda h, qs=qslot, ps=ps_, jj=jj, kk=kk: h.transpose(
                            pb[ps][:, 128 * jj:128 * (jj + 1)],
                            wqnd_bf[qs][:, 128 * kk:128 * (kk + 1)], id_sb[:]), tick=tk)
                        ev_tp = e or ev_tp
                    if done + cnt >= 22:
                        r_wqnd.release(ev_tp)
                    sc.wait_ev("scalar", ev_tp)
                    dstw = sb_pkc(wdT_q, KF, 22 * hh + done, 22 * hh + done + cnt,
                                  128 * c, 128 * (c + 1))
                    srcw = pb[ps_].rearrange("p (k c2) -> p k c2", k=8)[:, 0:cnt]
                    ev_ev = sc.op("scalar", lambda h, d=dstw, s=srcw: h.activation(
                        d, s, AF.Identity), tick=s_act)
                    r_tp.release(ev_ev)
                    done += cnt
        v_wdT = s_act.n
        sc.wait("tensor", s_act, v_wdT)
        ev_dn = None
        for t in range(T):
            hnext = ld_hq.issued % 2
            hslot, lhq = ld_hq.issue([(hq_in[hnext][:],
                                       hqT_d[t].rearrange("(k p) c -> p k c", p=128))])
            xnext = ld_x5.issued % 2
            xslot, lx5 = ld_x5.issue([(x5_in[xnext][:],
                                       x_in[128 * t:128 * (t + 1), 512 * q:512 * (q + 1)])])
            pslot = r_dnps.acquire("tensor")
            sc.wait("tensor", ld_hq.load[hslot], lhq)
            ev_dn = None
            for hk in range(KF):
                tk = s_pe if hk == KF - 1 else None
                e = sc.op("tensor", lambda h, ps=pslot, hk=hk, hs=hslot: h.matmul(
                    pf[ps][:], hq_in[hs][:, 128 * hk:128 * (hk + 1)],
                    wdT_q[:, 512 * hk:512 * (hk + 1)],
                    start=(hk == 0), stop=(hk == KF - 1)), tick=tk)
                ev_dn = e or ev_dn
            ld_hq.release(ev_dn)
            oslot = us_out.acquire("vector")
            sc.wait_ev("scalar", ev_dn)
            ev_f = sc.op("scalar", lambda h, ps=pslot, os=oslot, t=t: h.activation(
                ffnst[os][:], pf[ps][:], AF.Identity, scale=sd_all[:, t:t + 1]),
                tick=s_act)
            r_dnps.release(ev_f)
            sc.wait("vector", ld_x5.load[xslot], lx5)
            V5 = lambda fn, **kw: sc.op("vector", fn, **kw)
            ev_e1 = V5(lambda h, xs=xslot, os=oslot, t=t, q=q: h.scalar_tensor_tensor(
                outst[os][:], x5_in[xs][:], rinv1_all[:, t:t + 1],
                acc[:, D * t + 512 * q: D * t + 512 * (q + 1)], OP.mult, OP.add), tick=s_vec)
            ld_x5.release(ev_e1)
            ev_e2 = V5(lambda h, os=oslot, t=t: h.tensor_scalar(
                outst[os][:], outst[os][:], negmr_all[:, t:t + 1], None, OP.add), tick=s_vec)
            sc.wait_ev("vector", ev_f)
            ev_e3 = V5(lambda h, os=oslot: h.tensor_tensor(
                outst[os][:], outst[os][:], ffnst[os][:], OP.add), tick=s_vec)
            sc.wait_ev("scalar", ev_e3)
            us_out.issue(out[128 * t:128 * (t + 1), 512 * q:512 * (q + 1)], outst[oslot][:])
            if debug:
                sc.wait_ev("sync", ev_f)
                sc.op("sync", lambda h, os=oslot, t=t, q=q: h.dma_start(
                    dbg_ffn[128 * t:128 * (t + 1), 512 * q:512 * (q + 1)], ffnst[os][:]),
                    tick=us_out.st[oslot], inc=16)
        r_wdT.release(ev_dn)

    if debug:
        s_dbg = ESem("s_dbg")
        sc.wait("sync", s_vec, s_vec.n)
        sc.wait("sync", s_act, s_act.n)
        for dd, ss in [(dbg_p1, dbg1_sb[:]), (dbg_acc, acc[:]), (dbg_qT, qT[:]), (dbg_kT, kT[:]),
                       (dbg_st[:, 0:T], rinv1_all[:]), (dbg_st[:, T:2 * T], negmr_all[:]),
                       (dbg_st[:, 2 * T:3 * T], rs1_all[:]), (dbg_st[:, 3 * T:4 * T], rs2_all[:]),
                       (dbg_st[:, 4 * T:4 * T + 2], cmb[:]),
                       (dbg_st[:, 4 * T + 2:4 * T + 3], thr_gu[:]),
                       (dbg_st[:, 4 * T + 3:4 * T + 4], thr_dn[:])]:
            sc.op("sync", lambda h, d=dd, s2=ss: h.dma_start(d, s2, allow_slow_non_contiguous=True), tick=s_dbg, inc=16)
        sc.wait("sync", s_dbg, s_dbg.n)

    us_out.wait_all("scalar")

    es_e.close()
    es_r2.close()

    block = es_p.enter_context(nc.Block())
    for eng in ENGINES:
        thunks = sc.q[eng]
        if not thunks:
            continue
        getattr(block, eng)(lambda h, th=thunks: [f(h) for f in th])
    es_p.close()
    return nc


_NC_CACHE = {}


def _get_nc(debug=False):
    key = bool(debug)
    if key not in _NC_CACHE:
        _NC_CACHE[key] = build_program(debug=key)
    return _NC_CACHE[key]


def kernel(L0_attn_x, L0_attn_W_hidden, L0_attn_b_hidden, L0_attn_W_qk,
           L0_attn_b_qk, L0_attn_gamma_q, L0_attn_beta_q, L0_attn_gamma_k,
           L0_attn_beta_k, L0_attn_W_out, L0_attn_b_out, L0_ffn_w_gate_up,
           L0_ffn_w_down, L0_ffn_norm_weight):
    x = np.ascontiguousarray(np.asarray(L0_attn_x, dtype=np.float32))
    com = {
        "w_hid": np.ascontiguousarray(np.asarray(L0_attn_W_hidden, dtype=np.float32)),
        "b_hid": np.asarray(L0_attn_b_hidden, dtype=np.float32).reshape(1, HID2),
        "w_qk": np.ascontiguousarray(np.asarray(L0_attn_W_qk, dtype=np.float32)),
        "b_qk": np.asarray(L0_attn_b_qk, dtype=np.float32).reshape(QKD, 1),
        "g_q": np.asarray(L0_attn_gamma_q, dtype=np.float32).reshape(QKD, 1),
        "be_q": np.asarray(L0_attn_beta_q, dtype=np.float32).reshape(QKD, 1),
        "g_k": np.asarray(L0_attn_gamma_k, dtype=np.float32).reshape(QKD, 1),
        "be_k": np.asarray(L0_attn_beta_k, dtype=np.float32).reshape(QKD, 1),
        "w_out": np.ascontiguousarray(np.asarray(L0_attn_W_out, dtype=np.float32)),
        "b_out": np.asarray(L0_attn_b_out, dtype=np.float32).reshape(1, D),
        "w_gu": np.ascontiguousarray(np.asarray(L0_ffn_w_gate_up, dtype=np.float32)),
        "w_dn": np.ascontiguousarray(np.asarray(L0_ffn_w_down, dtype=np.float32)),
        "nw": np.asarray(L0_ffn_norm_weight, dtype=np.float32).reshape(1, D),
        "ident": np.eye(128, dtype=ml_dtypes.bfloat16),
        "ones_bf": np.ones((1, 128), dtype=ml_dtypes.bfloat16),
        "ones_f": np.ones((128, 1), dtype=np.float32),
        "ones_rf": np.ones((1, 128), dtype=np.float32),
    }
    in_maps = [dict(com, x=np.ascontiguousarray(x[b])) for b in range(8)]
    nc = _get_nc(debug=False)
    res = run_bass_kernel_spmd(nc, in_maps, core_ids=list(range(8)))
    return np.stack([np.asarray(r["out"], dtype=np.float32) for r in res.results], axis=0)
